# version 17
# baseline (speedup 1.0000x reference)
import math
import numpy as np
import ml_dtypes
from contextlib import ExitStack

import concourse.bass as bass
import concourse.mybir as mybir
from concourse.bass_utils import run_bass_kernel_spmd

F32 = mybir.dt.float32
BF16 = mybir.dt.bfloat16
I32 = mybir.dt.int32
AF = mybir.ActivationFunctionType
ALU = mybir.AluOpType
AX = mybir.AxisListType
AP = bass.AP

D = 2048
T = 2048
NSEQ = 2
NH = 16
HD = 128
HID = 5632
NDC = D // 128
NHO = HID // 128
NTT = T // 128
QSCALE = 1.0 / math.sqrt(HD)
NEG = -1.0e30
EPS = 1.0e-5
PACE = 0.0


def pace_n(n):
    return int(math.ceil(PACE * n))


class Sched:
    ENGS = ("sync", "pe", "act", "dve", "pool")

    def __init__(self):
        self.ops = {e: [] for e in self.ENGS}
        self.cnt = {}
        self.last_wait = {}

    def emit(self, eng, fn, waits=(), inc=None, inc_amt=1, force_waits=()):
        def collapse(wl):
            m = {}
            for w in wl:
                if w is None:
                    continue
                s_, v_ = w
                if v_ > m.get(s_, -1):
                    m[s_] = v_
            return [(s_, v_) for s_, v_ in m.items()]
        waits = collapse(waits)
        force_waits = collapse(force_waits)
        ws = []
        for w in force_waits:
            if w is None:
                continue
            sem, val = w
            k = (eng, sem)
            if self.last_wait.get(k, 0) >= val:
                continue
            ws.append((sem, val))
            self.last_wait[k] = val
        for w in waits:
            if w is None:
                continue
            sem, val = w
            if sem == eng:
                continue
            k = (eng, sem)
            if self.last_wait.get(k, 0) >= val:
                continue
            ws.append((sem, val))
            self.last_wait[k] = val
        ev = None
        if inc is not None:
            self.cnt[inc] = self.cnt.get(inc, 0) + inc_amt
            ev = (inc, self.cnt[inc])
        if inc is not None and eng in ("act", "dve"):
            self.ops[eng].append((fn, ws, None))
            self.ops[eng].append((lambda e: e.drain(), [], (inc, inc_amt)))
        else:
            self.ops[eng].append((fn, ws, (inc, inc_amt) if inc else None))
        return ev


def later(*evs):
    return [e for e in evs if e is not None]


def build_nc(max_phase=4, debug=0, nheads=NH, nqt=NTT):
    nc = bass.Bass("TRN2", target_bir_lowering=False)
    S = Sched()

    x_d = nc.dram_tensor("x", [NSEQ, T, D], F32, kind="ExternalInput")
    wqT_d = nc.dram_tensor("wqT", [D, D], F32, kind="ExternalInput")
    wkT_d = nc.dram_tensor("wkT", [D, D], F32, kind="ExternalInput")
    wvT_d = nc.dram_tensor("wvT", [D, D], F32, kind="ExternalInput")
    woT_d = nc.dram_tensor("woT", [D, D], F32, kind="ExternalInput")
    wgT_d = nc.dram_tensor("wgT", [D, HID], I32, kind="ExternalInput")
    wuT_d = nc.dram_tensor("wuT", [D, HID], I32, kind="ExternalInput")
    wdT_d = nc.dram_tensor("wdT", [HID, D], I32, kind="ExternalInput")
    sg_d = nc.dram_tensor("sg", [128, NHO], F32, kind="ExternalInput")
    su_d = nc.dram_tensor("su", [128, NHO], F32, kind="ExternalInput")
    sd_d = nc.dram_tensor("sd", [128, NDC], F32, kind="ExternalInput")
    id_d = nc.dram_tensor("ident", [128, 128], BF16, kind="ExternalInput")
    mk_d = nc.dram_tensor("mask", [128, 128], BF16, kind="ExternalInput")
    out_d = nc.dram_tensor("out", [NSEQ, T, D], F32, kind="ExternalOutput")

    wg_s = nc.dram_tensor("wg_s", [NHO, 128, NDC * 128], BF16)
    wu_s = nc.dram_tensor("wu_s", [NHO, 128, NDC * 128], BF16)
    wd_s = nc.dram_tensor("wd_s", [NDC, 128, NHO * 128], BF16)
    y_s = nc.dram_tensor("y_s", [NSEQ, T, D], F32)

    dbg = {}
    if debug >= 1:
        dbg["ln1"] = nc.dram_tensor("dbg_ln1", [128, NDC * T], BF16, kind="ExternalOutput")
    if debug >= 2:
        dbg["zt"] = nc.dram_tensor("dbg_zt", [128, NH * T], BF16, kind="ExternalOutput")
        dbg["qt"] = nc.dram_tensor("dbg_qt", [128, T], BF16, kind="ExternalOutput")
        dbg["kt"] = nc.dram_tensor("dbg_kt", [128, T], BF16, kind="ExternalOutput")
        dbg["vtm"] = nc.dram_tensor("dbg_vtm", [128, T], BF16, kind="ExternalOutput")
    if debug >= 3:
        dbg["y"] = nc.dram_tensor("dbg_y", [NSEQ, T, D], F32, kind="ExternalOutput")
    if debug >= 4:
        dbg["ln2"] = nc.dram_tensor("dbg_ln2", [128, NDC * 512], BF16, kind="ExternalOutput")
        dbg["ht"] = nc.dram_tensor("dbg_ht", [128, NHO * 512], BF16, kind="ExternalOutput")
        dbg["dt"] = nc.dram_tensor("dbg_dt", [128, NDC * 512], BF16, kind="ExternalOutput")

    es = ExitStack()

    def sb(name, shape, dt):
        return es.enter_context(nc.sbuf_tensor(name, shape, dt))

    def psum(name, shape, dt):
        return es.enter_context(nc.psum_tensor(name, shape, dt))

    A1 = sb("A1", [128, 34816], BF16)
    A2 = sb("A2", [128, 32768], BF16)
    A3 = sb("A3", [128, 20480], BF16)
    fA = sb("fA", [128, D], F32)
    fB = sb("fB", [128, D], F32)
    gtmp = sb("gtmp", [128, 512], F32)
    wint = [sb(f"wint{i}", [128, 1024], I32) for i in range(2)]
    wout = [sb(f"wout{i}", [128, 1024], BF16) for i in range(2)]
    ident = sb("identsb", [128, 128], BF16)
    mask = sb("masksb", [128, 128], BF16)
    zq_sb = sb("zqsb", [128, 128], BF16)
    sgc = sb("sgc", [128, NHO], F32)
    suc = sb("suc", [128, NHO], F32)
    sdc = sb("sdc", [128, NDC], F32)
    st6 = sb("st6", [128, 24], F32)
    mv = sb("mv", [128, 2], F32)
    veps = sb("veps", [128, 1], F32)
    sdv = sb("sdv", [128, 1], F32)
    rstd = sb("rstd", [128, 1], F32)
    nmr = sb("nmr", [128, 1], F32)
    negmx = sb("negmx", [128, 1], F32)
    sumr = sb("sumr", [128, 2], F32)
    rs = sb("rs", [128, 2], F32)

    QT = A3[:, 0:2048]
    KT = A3[:, 2048:4096]
    VT = A3[:, 4096:6144]
    Vtm = A3[:, 6144:8192]
    P_sb = A3[:, 8192:10240]
    Pn = A3[:, 10240:12288]
    PT = A3[:, 12288:14336]
    wq_sb = A3[:, 14336:16384]
    wk_sb = A3[:, 16384:18432]
    wv_sb = A3[:, 18432:20480]
    ln1_tm = A3[:, 0:2048]
    ln2tms = [A3[:, 2048:4096], A3[:, 4096:6144]]
    aoB = A3[:, 0:2048]

    ln1T = A1[:, 0:32768]
    wo_sb = A1[:, 0:32768]
    hT = A1[:, 0:22528]
    wgu4 = [A3[:, j * 4096:(j + 1) * 4096] for j in range(4)]
    dtms = [A3[:, 16384:18432], A3[:, 18432:20480]]
    wdsl = [A1[:, 22528:28160], A1[:, 28160:33792]]

    ZT = A2[:, 0:32768]

    pb_s = psum("pb_s", [128, 2048], F32)
    pb_pt = psum("pb_pt", [128, 2048], BF16)
    pb_q = [psum("pb_q0", [128, 512], F32), psum("pb_q1", [128, 512], F32)]
    pb_z = pb_q[0][:, 0:128]

    sems = {}
    semnames = ["dcs", "dfa", "dfb", "dwi", "dwo", "dc", "dbgs", "pe", "act", "dve", "pool"]
    semnames += ["dg%d%d" % (q, sl) for q in range(2) for sl in range(5)]
    semnames += ["dn%d%d" % (q, sl) for q in range(2) for sl in range(2)]
    for s_ in semnames:
        sems[s_] = es.enter_context(nc.semaphore(s_))

    RES = {}

    def mm(out_ap, lhsT_ap, rhs_ap, n, start, stop, waits=(), inc=None):
        ev = S.emit("pe", lambda e, o=out_ap, l=lhsT_ap, r=rhs_ap, st=start, sp=stop:
                    e.matmul(o, l, r, start=st, stop=sp), waits=waits, inc=inc)
        c = pace_n(n)
        if c:
            S.emit("pe", lambda e, c_=c: e.nop(cycle_cnt=c_))
        return ev

    def tr(out_ap, in_ap, waits=(), inc=None):
        ev = S.emit("pe", lambda e, o=out_ap, i=in_ap: e.transpose(o, i, ident[:]),
                    waits=waits, inc=inc)
        c = pace_n(128)
        if c:
            S.emit("pe", lambda e, c_=c: e.nop(cycle_cnt=c_))
        return ev

    def dma(eng, sem, dst, src, waits=(), force_waits=()):
        return S.emit(eng, lambda e, d=dst, s=src: e.dma_start(out=d, in_=s),
                      waits=waits, force_waits=force_waits, inc=sem, inc_amt=16)

    def ln_stats(src4, waits=()):
        for j, sap in enumerate(src4):
            S.emit("dve", lambda e, s_=sap, d_=st6[:, j * 6:(j + 1) * 6]: e.bn_stats(d_, s_),
                   waits=(waits if j == 0 else ()))
        S.emit("dve", lambda e: e.drain())
        S.emit("dve", lambda e: e.bn_aggr(mv[:], st6[:, 0:6 * len(src4)]))
        S.emit("dve", lambda e: e.drain())
        ev_veps = S.emit("dve", lambda e: e.tensor_scalar(veps[:], mv[:, 1:2], EPS, None, op0=ALU.add),
                         inc="dve")
        ev_sd = S.emit("act", lambda e: e.sqrt(sdv[:], veps[:]), waits=[ev_veps], inc="act")
        S.emit("dve", lambda e: e.reciprocal(rstd[:], sdv[:]), waits=[ev_sd])
        S.emit("dve", lambda e: e.drain())
        S.emit("dve", lambda e: e.tensor_scalar(nmr[:], mv[:, 0:1], rstd[:], -1.0,
                                                op0=ALU.mult, op1=ALU.mult))
        ev = S.emit("dve", lambda e: e.drain(), inc="dve")
        return ev

    phw_units = []

    def make_phw_units():
        for (src_d, dst_d) in ((wgT_d, wg_s), (wuT_d, wu_s)):
            for dc in range(NDC):
                for cg in range(HID // 1024 + 1):
                    wcols = min(1024, HID - cg * 1024)
                    src = AP(src_d, dc * 128 * HID + cg * 1024, [[HID, 128], [1, wcols]])
                    dst = AP(dst_d, 8 * cg * (128 * 2048) + dc * 128,
                             [[2048, 128], [128 * 2048, wcols // 128], [1, 128]])
                    phw_units.append((src, dst, wcols))
        for hoc in range(NHO):
            for dou in range(D // 1024):
                src = AP(wdT_d, hoc * 128 * D + dou * 1024, [[D, 128], [1, 1024]])
                dst = AP(wd_s, 8 * dou * (128 * 5632) + hoc * 128,
                         [[5632, 128], [128 * 5632, 8], [1, 128]])
                phw_units.append((src, dst, 1024))

    make_phw_units()
    phw_state = {"i": 0, "conv_ev": {}, "out_ev": {}}

    def pump_phw(k):
        for _ in range(k):
            i = phw_state["i"]
            if i >= len(phw_units):
                return
            phw_state["i"] = i + 1
            src, dst, wcols = phw_units[i]
            par = i % 2
            ev_in = dma("pool", "dwi", wint[par][:, 0:wcols], src,
                        waits=later(phw_state["conv_ev"].get(i - 2)))
            ev_cv = S.emit("pool",
                           lambda e, p=par, w=wcols: e.tensor_scalar(
                               wout[p][:, 0:w], wint[p][:, 0:w], -128.0, 1.0,
                               op0=ALU.add, op1=ALU.mult),
                           waits=later(ev_in, phw_state["out_ev"].get(i - 1)), inc="pool")
            phw_state["conv_ev"][i] = ev_cv
            ev_out = dma("pool", "dwo", dst, wout[par][:, 0:wcols], force_waits=later(ev_cv))
            phw_state["out_ev"][i] = ev_out

    def phw_done_ev():
        pump_phw(len(phw_units))
        n = len(phw_units)
        return later(phw_state["out_ev"].get(n - 1), phw_state["out_ev"].get(n - 2))

    ev_init = None
    if debug:
        for t_ in (A1, A2, A3):
            ev_init = S.emit("pool", lambda e, t__=t_: e.memset(t__[:, :], 0), inc="pool")

    for dst, src in ((ident, id_d), (mask, mk_d), (sgc, sg_d), (suc, su_d), (sdc, sd_d)):
        ev_consts = dma("sync", "dcs", dst[:], AP(src, 0, [[src.shape[1], 128], [1, src.shape[1]]]))

    def cast_head(h, waits):
        evs = []
        for (wsb, wdr) in ((wq_sb, wqT_d), (wk_sb, wkT_d), (wv_sb, wvT_d)):
            src = AP(wdr, h * 128, [[D, 128], [128 * D, NDC], [1, 128]])
            dst = AP(wsb.tensor, wsb.offset, [[20480, 128], [128, NDC], [1, 128]])
            evs.append(dma("pool", "dc", dst, src, waits=waits))
        return evs[-1]

    for s in range(NSEQ):
        RES["cast0"] = cast_head(0, later(RES.get("wgu_free"), RES.get("dtm0"),
                                          RES.get("dtm1")))
        for i in range(NTT):
            xb_ = fA if i % 2 == 0 else fB
            ev_x = dma("sync", "dfa" if i % 2 == 0 else "dfb", xb_[:],
                       AP(x_d, s * T * D + i * 128 * D, [[D, 128], [1, D]]),
                       waits=later(RES.get("fA" if i % 2 == 0 else "fB")))
            pump_phw(3)
            ev_nm = ln_stats([xb_[:, j * 512:(j + 1) * 512] for j in range(4)],
                             waits=later(ev_x))
            ev_norm = S.emit("dve", lambda e, b_=xb_: e.tensor_scalar(
                ln1_tm[:], b_[:], rstd[:], nmr[:], op0=ALU.mult, op1=ALU.add),
                waits=later(RES.get("ln1_tm"), RES.get("wgu_free"), ev_init), inc="dve")
            RES["fA" if i % 2 == 0 else "fB"] = ev_norm
            ev_tp = None
            for c in range(NDC):
                ev_tp = tr(pb_pt[:, c * 128:(c + 1) * 128], ln1_tm[:, c * 128:(c + 1) * 128],
                           waits=(later(ev_norm, RES.get("pb_pt"), ev_consts, ev_init)
                                  if c == 0 else ()),
                           inc=("pe" if c == NDC - 1 else None))
            RES["ln1_tm"] = ev_tp
            S.emit("act", lambda e, i_=i: e.copy(
                AP(ln1T.tensor, ln1T.offset + i_ * 128, [[34816, 128], [2048, 8], [1, 128]]),
                pb_pt[:, 0:1024]), waits=later(ev_tp))
            ev_ev = S.emit("act", lambda e, i_=i: e.copy(
                AP(ln1T.tensor, ln1T.offset + 8 * 2048 + i_ * 128,
                   [[34816, 128], [2048, 8], [1, 128]]),
                pb_pt[:, 1024:2048]), inc="act")
            RES["pb_pt"] = ev_ev
        ev_ln1_done = RES["pb_pt"]
        if debug >= 1 and s == 0:
            RES["dbg"] = dma("sync", "dbgs", AP(dbg["ln1"], 0, [[NDC * T, 128], [1, NDC * T]]),
                             ln1T, waits=later(ev_ln1_done))
        if max_phase < 2:
            break

        ev_cast = {}

        ev_cast[0] = RES["cast0"]
        for h in range(nheads):
            ev_qkv_banks = [RES.get("pb_q0"), RES.get("pb_q1")]
            ev_evacs = []
            for widx, (wsb, dstT, scale) in enumerate(
                    ((wq_sb, QT, QSCALE), (wk_sb, KT, None), (wv_sb, VT, None))):
                for tc in range(4):
                    b = (widx * 4 + tc) % 2
                    ev_g = None
                    for dc in range(NDC):
                        w0 = []
                        if dc == 0:
                            w0 = later(ev_cast[h], ev_qkv_banks[b])
                        ev_g = mm(pb_q[b][:], wsb[:, dc * 128:(dc + 1) * 128],
                                  ln1T[:, dc * 2048 + tc * 512: dc * 2048 + (tc + 1) * 512],
                                  512, dc == 0, dc == NDC - 1, waits=w0,
                                  inc=("pe" if dc == NDC - 1 else None))
                    if scale is None:
                        ev_e = S.emit("act", lambda e, d=dstT[:, tc * 512:(tc + 1) * 512],
                                      p=pb_q[b][:]: e.copy(d, p), waits=later(ev_g), inc="act")
                    else:
                        ev_e = S.emit("act", lambda e, d=dstT[:, tc * 512:(tc + 1) * 512],
                                      p=pb_q[b][:], sc=scale: e.activation(
                                          out=d, in_=p, func=AF.Copy, scale=sc),
                                      waits=later(ev_g), inc="act")
                    ev_qkv_banks[b] = ev_e
                    ev_evacs.append(ev_e)
            RES["pb_q0"] = ev_qkv_banks[0]
            RES["pb_q1"] = ev_qkv_banks[1]
            ev_vt_done = ev_evacs[-1]
            if h + 1 < nheads:
                ev_cast[h + 1] = cast_head(h + 1, later(ev_evacs[-1]))
            else:
                for g in range(4):
                    src = AP(woT_d, g * 4 * 128 * D, [[D, 128], [128 * D, 4], [1, D]])
                    dst = AP(A1, g * 4 * 2048, [[34816, 128], [2048, 4], [1, D]])
                    RES["wo"] = dma("pool", "dc", dst, src,
                                    waits=(later(ev_evacs[-1], RES.get("dbg"))
                                           if g == 0 else ()))
            ev_vtr = None
            for c in range(NDC):
                ev_vtr = tr(pb_pt[:, c * 128:(c + 1) * 128], VT[:, c * 128:(c + 1) * 128],
                            waits=(later(ev_vt_done, RES.get("pb_pt")) if c == 0 else ()),
                            inc=("pe" if c == NDC - 1 else None))
            ev_vtm = S.emit("act", lambda e: e.copy(Vtm[:], pb_pt[:]), waits=later(ev_vtr),
                            inc="act")
            RES["pb_pt"] = ev_vtm
            ev_qk_ready = later(ev_evacs[3], ev_evacs[7])
            PB = [P_sb, Pn]
            pend = None

            def emit_ptav(qt, ev_exp, ev_rc):
                buf = PB[qt % 2]
                par = qt % 2
                nch_k = qt + 1
                h1 = (nch_k + 1) // 2
                ev_pt1 = None
                for k in range(h1):
                    ev_pt1 = tr(pb_pt[:, k * 128:(k + 1) * 128], buf[:, k * 128:(k + 1) * 128],
                                waits=(later(ev_exp, RES.get("pb_pt")) if k == 0 else ()),
                                inc=("pe" if k == h1 - 1 else None))
                ev_pte1 = S.emit("dve", lambda e, k=h1 * 128: e.tensor_copy(
                    PT[:, 0:k], pb_pt[:, 0:k]), waits=later(ev_pt1), inc="dve")
                ev_pt2 = None
                for k in range(h1, nch_k):
                    ev_pt2 = tr(pb_pt[:, k * 128:(k + 1) * 128], buf[:, k * 128:(k + 1) * 128],
                                inc=("pe" if k == nch_k - 1 else None))
                RES["ptav%d" % (qt % 2)] = ev_pt2 if ev_pt2 is not None else ev_pt1
                ev_pte2 = ev_pte1
                if ev_pt2 is not None:
                    ev_pte2 = S.emit("dve", lambda e, a=h1 * 128, b=nch_k * 128: e.tensor_copy(
                        PT[:, a:b], pb_pt[:, a:b]), waits=later(ev_pt2), inc="dve")
                RES["pb_pt"] = ev_pte2
                ev_av = None
                for k in range(nch_k):
                    w0 = ()
                    if k == 0:
                        w0 = later(ev_pte1, ev_vtm, RES.get("pb_q0"))
                    elif k == h1:
                        w0 = later(ev_pte2)
                    ev_av = mm(pb_z, PT[:, k * 128:(k + 1) * 128], Vtm[:, k * 128:(k + 1) * 128],
                               128, k == 0, k == nch_k - 1,
                               waits=w0, inc=("pe" if k == nch_k - 1 else None))
                ev_zq = S.emit("dve", lambda e, p=par: e.tensor_scalar(
                    zq_sb[:], pb_z, rs[:, p:p + 1], None, op0=ALU.mult),
                    waits=later(ev_av, ev_rc, RES.get("zq_sb")), inc="dve")
                RES["pb_q0"] = ev_zq
                RES["rs%d" % par] = ev_zq
                ev_zt = tr(pb_pt[:, 0:128], zq_sb[:],
                           waits=later(ev_zq, RES.get("pb_pt")), inc="pe")
                ev_ze = S.emit("act", lambda e, h_=h, q=qt: e.copy(
                    ZT[:, h_ * 2048 + q * 128: h_ * 2048 + (q + 1) * 128], pb_pt[:, 0:128]),
                    waits=later(ev_zt), inc="act")
                RES["zq_sb"] = ev_zt
                RES["pb_pt"] = ev_ze

            for qt in range(nqt):
                kend = (qt + 1) * 128
                par = qt % 2
                buf = PB[par]
                nfull = qt // 4
                ev_s = None
                w0 = later(RES.get("pb_s"), *ev_qk_ready)
                for c in range(nfull):
                    ev_s = mm(pb_s[:, c * 512:(c + 1) * 512], QT[:, qt * 128:(qt + 1) * 128],
                              KT[:, c * 512:(c + 1) * 512], 512, True, True,
                              waits=(w0 if c == 0 else ()))
                rem = qt * 128 - nfull * 512
                if rem:
                    ev_s = mm(pb_s[:, nfull * 512: nfull * 512 + rem],
                              QT[:, qt * 128:(qt + 1) * 128],
                              KT[:, nfull * 512: nfull * 512 + rem], rem, True, True,
                              waits=(w0 if nfull == 0 else ()))
                mm(pb_s[:, qt * 128:kend], QT[:, qt * 128:(qt + 1) * 128],
                   KT[:, qt * 128:kend], 128, True, False,
                   waits=(w0 if (nfull == 0 and rem == 0) else ()))
                ev_s = S.emit("pe", lambda e, q=qt: e.matmul(
                    pb_s[:, q * 128:(q + 1) * 128], mask[:], ident[:],
                    start=False, stop=True), inc="pe")
                ev_mx = S.emit("dve", lambda e, k=kend: e.reduce_max(
                    negmx[:], pb_s[:, 0:k], axis=AX.X, negate=True),
                    waits=later(ev_s), inc="dve")
                ev_exp = S.emit("act", lambda e, k=kend, b=buf, p=par: e.activation(
                    out=b[:, 0:k], in_=pb_s[:, 0:k], func=AF.Exp, bias=negmx[:],
                    accum_out=sumr[:, p:p + 1]),
                    waits=later(ev_mx, RES.get("ptav%d" % par),
                                RES.get("sumr%d" % par)), inc="act")
                RES["pb_s"] = ev_exp
                if pend is not None:
                    emit_ptav(*pend)
                ev_rc = S.emit("dve", lambda e, p=par: e.reciprocal(
                    rs[:, p:p + 1], sumr[:, p:p + 1]),
                    waits=later(ev_exp, RES.get("rs%d" % par)), inc="dve")
                RES["sumr%d" % par] = ev_rc
                pend = (qt, ev_exp, ev_rc)
            emit_ptav(*pend)
            pend = None
            pump_phw(18)
        ev_zt_done = RES["pb_q0"]
        if debug >= 2 and s == 0:
            RES["dbg"] = dma("sync", "dbgs", AP(dbg["zt"], 0, [[NH * T, 128], [1, NH * T]]), ZT,
                             waits=later(ev_zt_done))
            RES["dbg"] = dma("sync", "dbgs", AP(dbg["qt"], 0, [[T, 128], [1, T]]), QT, waits=later(ev_zt_done))
            RES["dbg"] = dma("sync", "dbgs", AP(dbg["kt"], 0, [[T, 128], [1, T]]), KT, waits=later(ev_zt_done))
            RES["dbg"] = dma("sync", "dbgs", AP(dbg["vtm"], 0, [[T, 128], [1, T]]), Vtm, waits=later(ev_zt_done))
        if max_phase < 3:
            break

        ln2pend = None

        def flush_ln2():
            nonlocal ln2pend
            if ln2pend is None:
                return
            i_, ev_n2_ = ln2pend
            ln2pend = None
            p_ = i_ % 2
            ev_t2 = None
            for c in range(NDC):
                ev_t2 = tr(pb_pt[:, c * 128:(c + 1) * 128],
                           ln2tms[p_][:, c * 128:(c + 1) * 128],
                           waits=(later(ev_n2_, RES.get("pb_pt")) if c == 0 else ()),
                           inc=("pe" if c == NDC - 1 else None))
            RES["ln2tm%d" % p_] = ev_t2
            S.emit("act", lambda e, j=i_: e.copy(
                AP(A2, j * 128, [[32768, 128], [2048, 8], [1, 128]]),
                pb_pt[:, 0:1024]), waits=later(ev_t2))
            ev_e2 = S.emit("act", lambda e, j=i_: e.copy(
                AP(A2, 8 * 2048 + j * 128, [[32768, 128], [2048, 8], [1, 128]]),
                pb_pt[:, 1024:2048]), inc="act")
            RES["pb_pt"] = ev_e2
            RES["ln2T_done"] = ev_e2

        for i in range(NTT):
            ev_x3 = dma("sync", "dfa", fA[:], AP(x_d, s * T * D + i * 128 * D, [[D, 128], [1, D]]),
                        waits=later(RES.get("fA")))
            ev_o = None
            for h in range(NH):
                for col in range(4):
                    w0 = []
                    if h == 0 and col == 0:
                        w0 = later(RES.get("wo"), RES.get("pb_s"), ev_zt_done)
                    ev_o = mm(pb_s[:, col * 512:(col + 1) * 512],
                              ZT[:, h * 2048 + i * 128: h * 2048 + (i + 1) * 128],
                              wo_sb[:, h * 2048 + col * 512: h * 2048 + (col + 1) * 512],
                              512, h == 0, h == NH - 1, waits=w0,
                              inc=("pe" if (h == NH - 1 and col == 3) else None))
            ev_aoe = S.emit("act", lambda e: e.copy(aoB[:], pb_s[:]),
                            waits=later(ev_o, RES.get("aoB")), inc="act")
            RES["pb_s"] = ev_aoe
            ln_stats([aoB[:, j * 512:(j + 1) * 512] for j in range(4)], waits=later(ev_aoe))
            ev_xb = S.emit("dve", lambda e: e.tensor_scalar(
                fA[:], fA[:], nmr[:], None, op0=ALU.add), waits=later(ev_x3))
            S.emit("dve", lambda e: e.drain())
            ev_y = S.emit("dve", lambda e: e.scalar_tensor_tensor(
                fB[:], aoB[:], rstd[:], fA[:], op0=ALU.mult, op1=ALU.add),
                waits=later(RES.get("fB"), RES.get("fB2")), inc="dve")
            RES["aoB"] = ev_y
            RES["fA"] = ev_y
            ev_yd = dma("sync", "dfb", AP(y_s, s * T * D + i * 128 * D, [[D, 128], [1, D]]), fB[:],
                        waits=later(ev_y))
            RES["fB"] = ev_yd
            ln_stats([fB[:, j * 512:(j + 1) * 512] for j in range(4)], waits=later(ev_y))
            ev_n2 = S.emit("dve", lambda e, p=i % 2: e.tensor_scalar(
                ln2tms[p][:], fB[:], rstd[:], nmr[:], op0=ALU.mult, op1=ALU.add),
                waits=later(RES.get("ln2tm%d" % (i % 2))), inc="dve")
            RES["fB2"] = ev_n2
            flush_ln2()
            ln2pend = (i, ev_n2)
            pump_phw(12)
        flush_ln2()
        RES["wo_free"] = RES["ln2T_done"]
        if debug >= 3 and s == 0:
            RES["dbg"] = dma("sync", "dbgs", AP(dbg["y"], 0, [[1, NSEQ * T * D]]),
                             AP(y_s, 0, [[1, NSEQ * T * D]]), waits=later(RES.get("fB")))
        if max_phase < 4:
            break

        phw_wait = phw_done_ev() if s == 0 else []
        FB = [pb_q[0][:], pb_q[1][:], pb_s[:, 0:512], pb_s[:, 512:1024]]
        RES["fb0"] = RES.get("pb_q0")
        RES["fb1"] = RES.get("pb_q1")
        RES["fb2"] = RES.get("pb_s")
        RES["fb3"] = RES.get("pb_s")
        gdone_prev = {}
        ddone_prev = {}
        for tt in range(4):
            ev_ln2_done = RES.get("ln2T_done")
            qgu, qp = ("sync", 0) if tt % 2 == 0 else ("pool", 1)
            qdn = "pool" if tt % 2 == 0 else "sync"
            ev_gdone = {}
            ev_h_done = None
            for ho in range(NHO):
                sl = ho % 4
                gb, ub = 2 * (ho % 2), 2 * (ho % 2) + 1
                if ho < 4:
                    prev_guard = RES.get("wo_free") if tt == 0 else gdone_prev.get(NHO - 1)
                else:
                    prev_guard = None
                wts = later(*(phw_wait if (s == 0 and tt == 0 and ho < 4) else []),
                            ev_gdone.get(ho - 4), prev_guard)
                ev_wg = dma(qgu, "dg%d%d" % (qp, sl), AP(A3, sl * 4096, [[20480, 128], [1, 2048]]),
                            AP(wg_s, ho * 128 * 2048, [[2048, 128], [1, 2048]]), waits=wts)
                ev_wu = dma(qgu, "dg%d%d" % (qp, sl), AP(A3, sl * 4096 + 2048, [[20480, 128], [1, 2048]]),
                            AP(wu_s, ho * 128 * 2048, [[2048, 128], [1, 2048]]))
                for dc in range(NDC):
                    w0 = []
                    if dc == 0:
                        w0 = later(ev_wg, ev_wu, ev_ln2_done, RES.get("fb%d" % gb))
                    mm(FB[gb], wgu4[sl][:, dc * 128:(dc + 1) * 128],
                       A2[:, dc * 2048 + tt * 512: dc * 2048 + (tt + 1) * 512],
                       512, dc == 0, dc == NDC - 1, waits=w0)
                ev_gu = None
                for dc in range(NDC):
                    w0 = []
                    if dc == 0:
                        w0 = later(RES.get("fb%d" % ub))
                    ev_gu = mm(FB[ub], wgu4[sl][:, 2048 + dc * 128: 2048 + (dc + 1) * 128],
                               A2[:, dc * 2048 + tt * 512: dc * 2048 + (tt + 1) * 512],
                               512, dc == 0, dc == NDC - 1,
                               waits=w0, inc=("pe" if dc == NDC - 1 else None))
                ev_gdone[ho] = ev_gu
                ev_si = S.emit("act", lambda e, b=gb, o=ho: e.activation(
                    out=gtmp[:], in_=FB[b], func=AF.Silu, scale=sgc[:, o:o + 1]),
                    waits=later(ev_gu, RES.get("gtmp")), inc="act")
                RES["fb%d" % gb] = ev_si
                ev_h = S.emit("dve", lambda e, b=ub, o=ho: e.scalar_tensor_tensor(
                    hT[:, o * 512:(o + 1) * 512], FB[b], suc[:, o:o + 1], gtmp[:],
                    op0=ALU.mult, op1=ALU.mult), waits=later(ev_si), inc="dve")
                RES["fb%d" % ub] = ev_h
                RES["gtmp"] = ev_h
                ev_h_done = ev_h
            if debug >= 4 and s == 0 and tt == 0:
                RES["dbg"] = dma("sync", "dbgs", AP(dbg["ht"], 0, [[NHO * 512, 128], [1, NHO * 512]]),
                                 hT, waits=later(ev_h_done))
            ev_ddone = {}
            ev_dT_done = None
            for do in range(NDC):
                sl = do % 2
                b = do % 4
                if do >= 2:
                    slot_guard = ev_ddone.get(do - 2)
                elif tt > 0:
                    slot_guard = ddone_prev.get(NDC - 2 + do)
                else:
                    slot_guard = RES.get("wo_free")
                ev_wd = dma(qdn, "dn%d%d" % (qp, sl), AP(A1, 22528 + sl * 5632, [[34816, 128], [1, 5632]]),
                            AP(wd_s, do * 128 * 5632, [[5632, 128], [1, 5632]]),
                            waits=later(slot_guard))
                ev_d = None
                for hoc in range(NHO):
                    w0 = []
                    if hoc == 0:
                        w0 = later(ev_wd, RES.get("fb%d" % b), ev_h_done)
                    ev_d = mm(FB[b], wdsl[sl][:, hoc * 128:(hoc + 1) * 128],
                              hT[:, hoc * 512:(hoc + 1) * 512], 512, hoc == 0, hoc == NHO - 1,
                              waits=w0, inc=("pe" if hoc == NHO - 1 else None))
                ev_ddone[do] = ev_d
                ev_de = S.emit("act", lambda e, o=do, bb=b, t_=tt: e.activation(
                    out=A2[:, o * 2048 + t_ * 512: o * 2048 + (t_ + 1) * 512], in_=FB[bb],
                    func=AF.Copy,
                    scale=sdc[:, o:o + 1]), waits=later(ev_d), inc="act")
                RES["fb%d" % b] = ev_de
                ev_dT_done = ev_de
            RES["wd_read"] = ev_ddone.get(NDC - 1)
            for i2 in range(4):
                t0 = tt * 512 + i2 * 128
                dm = dtms[i2 % 2]
                ev_dtr = None
                for do in range(NDC):
                    ev_dtr = tr(pb_pt[:, do * 128:(do + 1) * 128],
                                A2[:, do * 2048 + tt * 512 + i2 * 128:
                                   do * 2048 + tt * 512 + (i2 + 1) * 128],
                                waits=(later(ev_dT_done, RES.get("pb_pt"))
                                       if do == 0 else ()),
                                inc=("pe" if do == NDC - 1 else None))
                ev_dtm = S.emit("act", lambda e, d_=dm: e.copy(d_[:], pb_pt[:]),
                                waits=later(ev_dtr, RES.get("dtm%d" % (i2 % 2))), inc="act")
                RES["pb_pt"] = ev_dtm
                ev_y42 = dma("sync", "dfa", fA[:], AP(y_s, s * T * D + t0 * D, [[D, 128], [1, D]]),
                             waits=later(RES.get("fA")))
                ln_stats([dm[:, j * 512:(j + 1) * 512] for j in range(4)],
                         waits=later(ev_dtm))
                S.emit("dve", lambda e: e.tensor_scalar(
                    fA[:], fA[:], nmr[:], None, op0=ALU.add), waits=later(ev_y42))
                S.emit("dve", lambda e: e.drain())
                ev_out = S.emit("dve", lambda e, d_=dm: e.scalar_tensor_tensor(
                    fB[:], d_[:], rstd[:], fA[:], op0=ALU.mult, op1=ALU.add),
                    waits=later(RES.get("fB")), inc="dve")
                RES["fA"] = ev_out
                RES["dtm%d" % (i2 % 2)] = ev_out
                ev_od = dma("sync", "dfb", AP(out_d, s * T * D + t0 * D, [[D, 128], [1, D]]), fB[:],
                            waits=later(ev_out))
                RES["fB"] = ev_od
            gdone_prev = ev_gdone
            ddone_prev = ev_ddone
        RES["pb_q0"] = RES["fb0"]
        RES["pb_q1"] = RES["fb1"]
        RES["pb_s"] = RES["fb3"]
        RES["wgu_free"] = gdone_prev.get(NHO - 1)

    S.emit("sync", lambda e: e.nop(),
           waits=[(s_, S.cnt[s_]) for s_ in semnames
                  if s_ not in ("pe", "act", "dve", "pool") and S.cnt.get(s_, 0) > 0])

    with es:
        with nc.Block() as block:
            secmap = {"sync": block.sync, "pe": block.tensor, "act": block.scalar,
                      "dve": block.vector, "pool": block.gpsimd}
            for eng_name in S.ENGS:
                ops = S.ops[eng_name]
                if not ops:
                    continue

                def make_body(ops_):
                    def body(eng):
                        for fn, ws, inc in ops_:
                            for sem, val in ws:
                                eng.wait_ge(sems[sem], val)
                            inst = fn(eng)
                            if inc is not None:
                                sem, amt = inc
                                if inst is None:
                                    inst = eng.nop()
                                inst.then_inc(sems[sem], amt)
                    return body

                secmap[eng_name](make_body(ops))
    return nc


_BUILD_KW = {"max_phase": 4, "debug": 0, "nheads": NH, "nqt": NTT}


def kernel(L0_attn_x, L0_attn_Wq, L0_attn_Wk, L0_attn_Wv, L0_attn_Wo,
           L0_ffn_wq_gate, L0_ffn_s_gate, L0_ffn_wq_up, L0_ffn_s_up,
           L0_ffn_wq_down, L0_ffn_s_down):
    x = np.ascontiguousarray(np.asarray(L0_attn_x, dtype=np.float32))
    wqT = np.ascontiguousarray(np.asarray(L0_attn_Wq, dtype=np.float32).T)
    wkT = np.ascontiguousarray(np.asarray(L0_attn_Wk, dtype=np.float32).T)
    wvT = np.ascontiguousarray(np.asarray(L0_attn_Wv, dtype=np.float32).T)
    woT = np.ascontiguousarray(np.asarray(L0_attn_Wo, dtype=np.float32).T)
    wgT = np.ascontiguousarray(np.asarray(L0_ffn_wq_gate, dtype=np.int32).T)
    wuT = np.ascontiguousarray(np.asarray(L0_ffn_wq_up, dtype=np.int32).T)
    wdT = np.ascontiguousarray(np.asarray(L0_ffn_wq_down, dtype=np.int32).T)
    sg = np.ascontiguousarray(np.asarray(L0_ffn_s_gate, dtype=np.float32).reshape(NHO, 128).T)
    su = np.ascontiguousarray(np.asarray(L0_ffn_s_up, dtype=np.float32).reshape(NHO, 128).T)
    sd = np.ascontiguousarray(np.asarray(L0_ffn_s_down, dtype=np.float32).reshape(NDC, 128).T)
    ident = np.eye(128, dtype=ml_dtypes.bfloat16)
    mask = np.ascontiguousarray(
        np.triu(np.full((128, 128), NEG, dtype=np.float32), k=1).T.astype(ml_dtypes.bfloat16))

    nc = build_nc(**_BUILD_KW)
    in_maps = []
    for c in range(8):
        in_maps.append({
            "x": x[2 * c:2 * c + 2], "wqT": wqT, "wkT": wkT, "wvT": wvT, "woT": woT,
            "wgT": wgT, "wuT": wuT, "wdT": wdT, "sg": sg, "su": su, "sd": sd,
            "ident": ident, "mask": mask,
        })
    res = run_bass_kernel_spmd(nc, in_maps, core_ids=list(range(8)))
    if _BUILD_KW["debug"] or _BUILD_KW["max_phase"] < 4:
        return res
    return np.concatenate([r["out"] for r in res.results], axis=0)


# version 18
# speedup vs baseline: 1.0361x; 1.0361x over previous
import math
import numpy as np
import ml_dtypes
from contextlib import ExitStack

import concourse.bass as bass
import concourse.mybir as mybir
from concourse.bass_utils import run_bass_kernel_spmd

F32 = mybir.dt.float32
BF16 = mybir.dt.bfloat16
I32 = mybir.dt.int32
AF = mybir.ActivationFunctionType
ALU = mybir.AluOpType
AX = mybir.AxisListType
AP = bass.AP

D = 2048
T = 2048
NSEQ = 2
NH = 16
HD = 128
HID = 5632
NDC = D // 128
NHO = HID // 128
NTT = T // 128
QSCALE = 1.0 / math.sqrt(HD)
NEG = -1.0e30
EPS = 1.0e-5
PACE = 0.0


def pace_n(n):
    return int(math.ceil(PACE * n))


class Sched:
    ENGS = ("sync", "pe", "act", "dve", "pool")

    def __init__(self):
        self.ops = {e: [] for e in self.ENGS}
        self.cnt = {}
        self.last_wait = {}

    def emit(self, eng, fn, waits=(), inc=None, inc_amt=1, force_waits=()):
        def collapse(wl):
            m = {}
            for w in wl:
                if w is None:
                    continue
                s_, v_ = w
                if v_ > m.get(s_, -1):
                    m[s_] = v_
            return [(s_, v_) for s_, v_ in m.items()]
        waits = collapse(waits)
        force_waits = collapse(force_waits)
        ws = []
        for w in force_waits:
            if w is None:
                continue
            sem, val = w
            k = (eng, sem)
            if self.last_wait.get(k, 0) >= val:
                continue
            ws.append((sem, val))
            self.last_wait[k] = val
        for w in waits:
            if w is None:
                continue
            sem, val = w
            if sem == eng:
                continue
            k = (eng, sem)
            if self.last_wait.get(k, 0) >= val:
                continue
            ws.append((sem, val))
            self.last_wait[k] = val
        ev = None
        if inc is not None:
            self.cnt[inc] = self.cnt.get(inc, 0) + inc_amt
            ev = (inc, self.cnt[inc])
        if inc is not None and eng in ("act", "dve"):
            self.ops[eng].append((fn, ws, None))
            self.ops[eng].append((lambda e: e.drain(), [], (inc, inc_amt)))
        else:
            self.ops[eng].append((fn, ws, (inc, inc_amt) if inc else None))
        return ev


def later(*evs):
    return [e for e in evs if e is not None]


def build_nc(max_phase=4, debug=0, nheads=NH, nqt=NTT):
    nc = bass.Bass("TRN2", target_bir_lowering=False)
    S = Sched()

    x_d = nc.dram_tensor("x", [NSEQ, T, D], F32, kind="ExternalInput")
    wqT_d = nc.dram_tensor("wqT", [D, D], F32, kind="ExternalInput")
    wkT_d = nc.dram_tensor("wkT", [D, D], F32, kind="ExternalInput")
    wvT_d = nc.dram_tensor("wvT", [D, D], F32, kind="ExternalInput")
    woT_d = nc.dram_tensor("woT", [D, D], F32, kind="ExternalInput")
    wgT_d = nc.dram_tensor("wgT", [D, HID], I32, kind="ExternalInput")
    wuT_d = nc.dram_tensor("wuT", [D, HID], I32, kind="ExternalInput")
    wdT_d = nc.dram_tensor("wdT", [HID, D], I32, kind="ExternalInput")
    sg_d = nc.dram_tensor("sg", [128, NHO], F32, kind="ExternalInput")
    su_d = nc.dram_tensor("su", [128, NHO], F32, kind="ExternalInput")
    sd_d = nc.dram_tensor("sd", [128, NDC], F32, kind="ExternalInput")
    id_d = nc.dram_tensor("ident", [128, 128], BF16, kind="ExternalInput")
    mk_d = nc.dram_tensor("mask", [128, 128], BF16, kind="ExternalInput")
    out_d = nc.dram_tensor("out", [NSEQ, T, D], F32, kind="ExternalOutput")

    wg_s = nc.dram_tensor("wg_s", [NHO, 128, NDC * 128], BF16)
    wu_s = nc.dram_tensor("wu_s", [NHO, 128, NDC * 128], BF16)
    wd_s = nc.dram_tensor("wd_s", [NDC, 128, NHO * 128], BF16)
    y_s = nc.dram_tensor("y_s", [NSEQ, T, D], F32)

    dbg = {}
    if debug >= 1:
        dbg["ln1"] = nc.dram_tensor("dbg_ln1", [128, NDC * T], BF16, kind="ExternalOutput")
    if debug >= 2:
        dbg["zt"] = nc.dram_tensor("dbg_zt", [128, NH * T], BF16, kind="ExternalOutput")
        dbg["qt"] = nc.dram_tensor("dbg_qt", [128, T], BF16, kind="ExternalOutput")
        dbg["kt"] = nc.dram_tensor("dbg_kt", [128, T], BF16, kind="ExternalOutput")
        dbg["vtm"] = nc.dram_tensor("dbg_vtm", [128, T], BF16, kind="ExternalOutput")
    if debug >= 3:
        dbg["y"] = nc.dram_tensor("dbg_y", [NSEQ, T, D], F32, kind="ExternalOutput")
    if debug >= 4:
        dbg["ln2"] = nc.dram_tensor("dbg_ln2", [128, NDC * 512], BF16, kind="ExternalOutput")
        dbg["ht"] = nc.dram_tensor("dbg_ht", [128, NHO * 512], BF16, kind="ExternalOutput")
        dbg["dt"] = nc.dram_tensor("dbg_dt", [128, NDC * 512], BF16, kind="ExternalOutput")

    es = ExitStack()

    def sb(name, shape, dt):
        return es.enter_context(nc.sbuf_tensor(name, shape, dt))

    def psum(name, shape, dt):
        return es.enter_context(nc.psum_tensor(name, shape, dt))

    A1 = sb("A1", [128, 34816], BF16)
    A2 = sb("A2", [128, 32768], BF16)
    A3 = sb("A3", [128, 20480], BF16)
    fA = sb("fA", [128, D], F32)
    fB = sb("fB", [128, D], F32)
    gtmp = sb("gtmp", [128, 512], F32)
    wint = [sb(f"wint{i}", [128, 1024], I32) for i in range(2)]
    wout = [sb(f"wout{i}", [128, 1024], BF16) for i in range(2)]
    ident = sb("identsb", [128, 128], BF16)
    mask = sb("masksb", [128, 128], BF16)
    zq_sb = sb("zqsb", [128, 128], BF16)
    sgc = sb("sgc", [128, NHO], F32)
    suc = sb("suc", [128, NHO], F32)
    sdc = sb("sdc", [128, NDC], F32)
    st6 = sb("st6", [128, 24], F32)
    mv = sb("mv", [128, 2], F32)
    veps = sb("veps", [128, 1], F32)
    sdv = sb("sdv", [128, 1], F32)
    rstd = sb("rstd", [128, 1], F32)
    nmr = sb("nmr", [128, 1], F32)
    negmx = sb("negmx", [128, 1], F32)
    sumr = sb("sumr", [128, 2], F32)
    rs = sb("rs", [128, 2], F32)

    QT = A3[:, 0:2048]
    KT = A3[:, 2048:4096]
    VT = A3[:, 4096:6144]
    Vtm = A3[:, 6144:8192]
    P_sb = A3[:, 8192:10240]
    Pn = A3[:, 10240:12288]
    PT = A3[:, 12288:14336]
    wq_sb = A3[:, 14336:16384]
    wk_sb = A3[:, 16384:18432]
    wv_sb = A3[:, 18432:20480]
    ln1tms = [A3[:, 0:2048], A3[:, 2048:4096]]
    ln2tms = [A3[:, 2048:4096], A3[:, 4096:6144]]
    aoB = A3[:, 0:2048]

    ln1T = A1[:, 0:32768]
    wo_sb = A1[:, 0:32768]
    hT = A1[:, 0:22528]
    wgu4 = [A3[:, j * 4096:(j + 1) * 4096] for j in range(4)]
    dtms = [A3[:, 16384:18432], A3[:, 18432:20480]]
    wdsl = [A1[:, 22528:28160], A1[:, 28160:33792]]

    ZT = A2[:, 0:32768]

    pb_s = psum("pb_s", [128, 2048], F32)
    pb_pt = psum("pb_pt", [128, 2048], BF16)
    pb_q = [psum("pb_q0", [128, 512], F32), psum("pb_q1", [128, 512], F32)]
    pb_z = pb_q[0][:, 0:128]

    sems = {}
    semnames = ["dcs", "dfa", "dfb", "dwi", "dwo", "dc", "dbgs", "pe", "act", "dve", "pool"]
    semnames += ["dg%d%d" % (q, sl) for q in range(2) for sl in range(5)]
    semnames += ["dn%d%d" % (q, sl) for q in range(2) for sl in range(2)]
    for s_ in semnames:
        sems[s_] = es.enter_context(nc.semaphore(s_))

    RES = {}

    def mm(out_ap, lhsT_ap, rhs_ap, n, start, stop, waits=(), inc=None):
        ev = S.emit("pe", lambda e, o=out_ap, l=lhsT_ap, r=rhs_ap, st=start, sp=stop:
                    e.matmul(o, l, r, start=st, stop=sp), waits=waits, inc=inc)
        c = pace_n(n)
        if c:
            S.emit("pe", lambda e, c_=c: e.nop(cycle_cnt=c_))
        return ev

    def tr(out_ap, in_ap, waits=(), inc=None):
        ev = S.emit("pe", lambda e, o=out_ap, i=in_ap: e.transpose(o, i, ident[:]),
                    waits=waits, inc=inc)
        c = pace_n(128)
        if c:
            S.emit("pe", lambda e, c_=c: e.nop(cycle_cnt=c_))
        return ev

    def dma(eng, sem, dst, src, waits=(), force_waits=()):
        return S.emit(eng, lambda e, d=dst, s=src: e.dma_start(out=d, in_=s),
                      waits=waits, force_waits=force_waits, inc=sem, inc_amt=16)

    def ln_stats(src4, waits=()):
        for j, sap in enumerate(src4):
            S.emit("dve", lambda e, s_=sap, d_=st6[:, j * 6:(j + 1) * 6]: e.bn_stats(d_, s_),
                   waits=(waits if j == 0 else ()))
        S.emit("dve", lambda e: e.drain())
        S.emit("dve", lambda e: e.bn_aggr(mv[:], st6[:, 0:6 * len(src4)]))
        S.emit("dve", lambda e: e.drain())
        ev_veps = S.emit("dve", lambda e: e.tensor_scalar(veps[:], mv[:, 1:2], EPS, None, op0=ALU.add),
                         inc="dve")
        ev_sd = S.emit("act", lambda e: e.sqrt(sdv[:], veps[:]), waits=[ev_veps], inc="act")
        S.emit("dve", lambda e: e.reciprocal(rstd[:], sdv[:]), waits=[ev_sd])
        S.emit("dve", lambda e: e.drain())
        S.emit("dve", lambda e: e.tensor_scalar(nmr[:], mv[:, 0:1], rstd[:], -1.0,
                                                op0=ALU.mult, op1=ALU.mult))
        ev = S.emit("dve", lambda e: e.drain(), inc="dve")
        return ev

    phw_units = []

    def make_phw_units():
        for (src_d, dst_d) in ((wgT_d, wg_s), (wuT_d, wu_s)):
            for dc in range(NDC):
                for cg in range(HID // 1024 + 1):
                    wcols = min(1024, HID - cg * 1024)
                    src = AP(src_d, dc * 128 * HID + cg * 1024, [[HID, 128], [1, wcols]])
                    dst = AP(dst_d, 8 * cg * (128 * 2048) + dc * 128,
                             [[2048, 128], [128 * 2048, wcols // 128], [1, 128]])
                    phw_units.append((src, dst, wcols))
        for hoc in range(NHO):
            for dou in range(D // 1024):
                src = AP(wdT_d, hoc * 128 * D + dou * 1024, [[D, 128], [1, 1024]])
                dst = AP(wd_s, 8 * dou * (128 * 5632) + hoc * 128,
                         [[5632, 128], [128 * 5632, 8], [1, 128]])
                phw_units.append((src, dst, 1024))

    make_phw_units()
    phw_state = {"i": 0, "conv_ev": {}, "out_ev": {}}

    def pump_phw(k):
        for _ in range(k):
            i = phw_state["i"]
            if i >= len(phw_units):
                return
            phw_state["i"] = i + 1
            src, dst, wcols = phw_units[i]
            par = i % 2
            ev_in = dma("pool", "dwi", wint[par][:, 0:wcols], src,
                        waits=later(phw_state["conv_ev"].get(i - 2)))
            ev_cv = S.emit("pool",
                           lambda e, p=par, w=wcols: e.tensor_scalar(
                               wout[p][:, 0:w], wint[p][:, 0:w], -128.0, 1.0,
                               op0=ALU.add, op1=ALU.mult),
                           waits=later(ev_in, phw_state["out_ev"].get(i - 1)), inc="pool")
            phw_state["conv_ev"][i] = ev_cv
            ev_out = dma("pool", "dwo", dst, wout[par][:, 0:wcols], force_waits=later(ev_cv))
            phw_state["out_ev"][i] = ev_out

    def phw_done_ev():
        pump_phw(len(phw_units))
        n = len(phw_units)
        return later(phw_state["out_ev"].get(n - 1), phw_state["out_ev"].get(n - 2))

    ev_init = None
    if debug:
        for t_ in (A1, A2, A3):
            ev_init = S.emit("pool", lambda e, t__=t_: e.memset(t__[:, :], 0), inc="pool")

    for dst, src in ((ident, id_d), (mask, mk_d), (sgc, sg_d), (suc, su_d), (sdc, sd_d)):
        ev_consts = dma("sync", "dcs", dst[:], AP(src, 0, [[src.shape[1], 128], [1, src.shape[1]]]))

    def cast_head(h, waits):
        ev = None
        for (wsb, wdr) in ((wq_sb, wqT_d), (wk_sb, wkT_d), (wv_sb, wvT_d)):
            for g in range(2):
                src = AP(wdr, h * 128 + g * 8 * 128 * D, [[D, 128], [128 * D, 8], [1, 128]])
                dst = AP(wsb.tensor, wsb.offset + g * 8 * 128,
                         [[20480, 128], [128, 8], [1, 128]])
                ev = dma("pool", "dc", dst, src, waits=(waits if g == 0 else ()))
        return ev

    for s in range(NSEQ):
        RES["cast0"] = cast_head(0, later(RES.get("wgu_free"), RES.get("dtm0"),
                                          RES.get("dtm1")))
        ln1pend = None

        def flush_ln1():
            nonlocal ln1pend
            if ln1pend is None:
                return
            i_, ev_nm_ = ln1pend
            ln1pend = None
            p_ = i_ % 2
            ev_tp = None
            for c in range(NDC):
                ev_tp = tr(pb_pt[:, c * 128:(c + 1) * 128],
                           ln1tms[p_][:, c * 128:(c + 1) * 128],
                           waits=(later(ev_nm_, RES.get("pb_pt"), ev_consts, ev_init)
                                  if c == 0 else ()),
                           inc=("pe" if c == NDC - 1 else None))
            RES["ln1tm%d" % p_] = ev_tp
            S.emit("act", lambda e, j=i_: e.copy(
                AP(ln1T.tensor, ln1T.offset + j * 128, [[34816, 128], [2048, 8], [1, 128]]),
                pb_pt[:, 0:1024]), waits=later(ev_tp))
            ev_ev = S.emit("act", lambda e, j=i_: e.copy(
                AP(ln1T.tensor, ln1T.offset + 8 * 2048 + j * 128,
                   [[34816, 128], [2048, 8], [1, 128]]),
                pb_pt[:, 1024:2048]), inc="act")
            RES["pb_pt"] = ev_ev

        for i in range(NTT):
            xb_ = fA if i % 2 == 0 else fB
            ev_x = dma("sync", "dfa" if i % 2 == 0 else "dfb", xb_[:],
                       AP(x_d, s * T * D + i * 128 * D, [[D, 128], [1, D]]),
                       waits=later(RES.get("fA" if i % 2 == 0 else "fB")))
            pump_phw(3)
            ev_nm = ln_stats([xb_[:, j * 512:(j + 1) * 512] for j in range(4)],
                             waits=later(ev_x))
            ev_norm = S.emit("dve", lambda e, b_=xb_, p=i % 2: e.tensor_scalar(
                ln1tms[p][:], b_[:], rstd[:], nmr[:], op0=ALU.mult, op1=ALU.add),
                waits=later(RES.get("ln1tm%d" % (i % 2)), RES.get("wgu_free"), ev_init),
                inc="dve")
            RES["fA" if i % 2 == 0 else "fB"] = ev_norm
            flush_ln1()
            ln1pend = (i, ev_norm)
        flush_ln1()
        ev_ln1_done = RES["pb_pt"]
        if debug >= 1 and s == 0:
            RES["dbg"] = dma("sync", "dbgs", AP(dbg["ln1"], 0, [[NDC * T, 128], [1, NDC * T]]),
                             ln1T, waits=later(ev_ln1_done))
        if max_phase < 2:
            break

        ev_cast = {}

        ev_cast[0] = RES["cast0"]
        for h in range(nheads):
            ev_qkv_banks = [RES.get("pb_q0"), RES.get("pb_q1")]
            ev_evacs = []
            for widx, (wsb, dstT, scale) in enumerate(
                    ((wq_sb, QT, QSCALE), (wk_sb, KT, None), (wv_sb, VT, None))):
                for tc in range(4):
                    b = (widx * 4 + tc) % 2
                    ev_g = None
                    for dc in range(NDC):
                        w0 = []
                        if dc == 0:
                            w0 = later(ev_cast[h], ev_qkv_banks[b])
                        ev_g = mm(pb_q[b][:], wsb[:, dc * 128:(dc + 1) * 128],
                                  ln1T[:, dc * 2048 + tc * 512: dc * 2048 + (tc + 1) * 512],
                                  512, dc == 0, dc == NDC - 1, waits=w0,
                                  inc=("pe" if dc == NDC - 1 else None))
                    if scale is None:
                        ev_e = S.emit("act", lambda e, d=dstT[:, tc * 512:(tc + 1) * 512],
                                      p=pb_q[b][:]: e.copy(d, p), waits=later(ev_g), inc="act")
                    else:
                        ev_e = S.emit("act", lambda e, d=dstT[:, tc * 512:(tc + 1) * 512],
                                      p=pb_q[b][:], sc=scale: e.activation(
                                          out=d, in_=p, func=AF.Copy, scale=sc),
                                      waits=later(ev_g), inc="act")
                    ev_qkv_banks[b] = ev_e
                    ev_evacs.append(ev_e)
            RES["pb_q0"] = ev_qkv_banks[0]
            RES["pb_q1"] = ev_qkv_banks[1]
            ev_vt_done = ev_evacs[-1]
            if h + 1 < nheads:
                ev_cast[h + 1] = cast_head(h + 1, later(ev_evacs[-1]))
            else:
                for g in range(4):
                    src = AP(woT_d, g * 4 * 128 * D, [[D, 128], [128 * D, 4], [1, D]])
                    dst = AP(A1, g * 4 * 2048, [[34816, 128], [2048, 4], [1, D]])
                    RES["wo"] = dma("pool", "dc", dst, src,
                                    waits=(later(ev_evacs[-1], RES.get("dbg"))
                                           if g == 0 else ()))
            ev_vtr = None
            for c in range(NDC):
                ev_vtr = tr(pb_pt[:, c * 128:(c + 1) * 128], VT[:, c * 128:(c + 1) * 128],
                            waits=(later(ev_vt_done, RES.get("pb_pt")) if c == 0 else ()),
                            inc=("pe" if c == NDC - 1 else None))
            ev_vtm = S.emit("act", lambda e: e.copy(Vtm[:], pb_pt[:]), waits=later(ev_vtr),
                            inc="act")
            RES["pb_pt"] = ev_vtm
            ev_qk_ready = later(ev_evacs[3], ev_evacs[7])
            PB = [P_sb, Pn]
            pend = None

            def emit_ptav(qt, ev_exp, ev_rc):
                buf = PB[qt % 2]
                par = qt % 2
                nch_k = qt + 1
                h1 = (nch_k + 1) // 2
                ev_pt1 = None
                for k in range(h1):
                    ev_pt1 = tr(pb_pt[:, k * 128:(k + 1) * 128], buf[:, k * 128:(k + 1) * 128],
                                waits=(later(ev_exp, RES.get("pb_pt")) if k == 0 else ()),
                                inc=("pe" if k == h1 - 1 else None))
                ev_pte1 = S.emit("dve", lambda e, k=h1 * 128: e.tensor_copy(
                    PT[:, 0:k], pb_pt[:, 0:k]), waits=later(ev_pt1), inc="dve")
                ev_pt2 = None
                for k in range(h1, nch_k):
                    ev_pt2 = tr(pb_pt[:, k * 128:(k + 1) * 128], buf[:, k * 128:(k + 1) * 128],
                                inc=("pe" if k == nch_k - 1 else None))
                RES["ptav%d" % (qt % 2)] = ev_pt2 if ev_pt2 is not None else ev_pt1
                ev_pte2 = ev_pte1
                if ev_pt2 is not None:
                    ev_pte2 = S.emit("dve", lambda e, a=h1 * 128, b=nch_k * 128: e.tensor_copy(
                        PT[:, a:b], pb_pt[:, a:b]), waits=later(ev_pt2), inc="dve")
                RES["pb_pt"] = ev_pte2
                ev_av = None
                for k in range(nch_k):
                    w0 = ()
                    if k == 0:
                        w0 = later(ev_pte1, ev_vtm, RES.get("pb_q0"))
                    elif k == h1:
                        w0 = later(ev_pte2)
                    ev_av = mm(pb_z, PT[:, k * 128:(k + 1) * 128], Vtm[:, k * 128:(k + 1) * 128],
                               128, k == 0, k == nch_k - 1,
                               waits=w0, inc=("pe" if k == nch_k - 1 else None))
                ev_zq = S.emit("dve", lambda e, p=par: e.tensor_scalar(
                    zq_sb[:], pb_z, rs[:, p:p + 1], None, op0=ALU.mult),
                    waits=later(ev_av, ev_rc, RES.get("zq_sb")), inc="dve")
                RES["pb_q0"] = ev_zq
                RES["rs%d" % par] = ev_zq
                ev_zt = tr(pb_pt[:, 0:128], zq_sb[:],
                           waits=later(ev_zq, RES.get("pb_pt")), inc="pe")
                ev_ze = S.emit("act", lambda e, h_=h, q=qt: e.copy(
                    ZT[:, h_ * 2048 + q * 128: h_ * 2048 + (q + 1) * 128], pb_pt[:, 0:128]),
                    waits=later(ev_zt), inc="act")
                RES["zq_sb"] = ev_zt
                RES["pb_pt"] = ev_ze

            for qt in range(nqt):
                kend = (qt + 1) * 128
                par = qt % 2
                buf = PB[par]
                nfull = qt // 4
                ev_s = None
                w0 = later(RES.get("pb_s"), *ev_qk_ready)
                for c in range(nfull):
                    ev_s = mm(pb_s[:, c * 512:(c + 1) * 512], QT[:, qt * 128:(qt + 1) * 128],
                              KT[:, c * 512:(c + 1) * 512], 512, True, True,
                              waits=(w0 if c == 0 else ()))
                rem = qt * 128 - nfull * 512
                if rem:
                    ev_s = mm(pb_s[:, nfull * 512: nfull * 512 + rem],
                              QT[:, qt * 128:(qt + 1) * 128],
                              KT[:, nfull * 512: nfull * 512 + rem], rem, True, True,
                              waits=(w0 if nfull == 0 else ()))
                mm(pb_s[:, qt * 128:kend], QT[:, qt * 128:(qt + 1) * 128],
                   KT[:, qt * 128:kend], 128, True, False,
                   waits=(w0 if (nfull == 0 and rem == 0) else ()))
                ev_s = S.emit("pe", lambda e, q=qt: e.matmul(
                    pb_s[:, q * 128:(q + 1) * 128], mask[:], ident[:],
                    start=False, stop=True), inc="pe")
                ev_mx = S.emit("dve", lambda e, k=kend: e.reduce_max(
                    negmx[:], pb_s[:, 0:k], axis=AX.X, negate=True),
                    waits=later(ev_s), inc="dve")
                ev_exp = S.emit("act", lambda e, k=kend, b=buf, p=par: e.activation(
                    out=b[:, 0:k], in_=pb_s[:, 0:k], func=AF.Exp, bias=negmx[:],
                    accum_out=sumr[:, p:p + 1]),
                    waits=later(ev_mx, RES.get("ptav%d" % par),
                                RES.get("sumr%d" % par)), inc="act")
                RES["pb_s"] = ev_exp
                if pend is not None:
                    emit_ptav(*pend)
                ev_rc = S.emit("dve", lambda e, p=par: e.reciprocal(
                    rs[:, p:p + 1], sumr[:, p:p + 1]),
                    waits=later(ev_exp, RES.get("rs%d" % par)), inc="dve")
                RES["sumr%d" % par] = ev_rc
                pend = (qt, ev_exp, ev_rc)
            emit_ptav(*pend)
            pend = None
            pump_phw(14)
        ev_zt_done = RES["pb_q0"]
        if debug >= 2 and s == 0:
            RES["dbg"] = dma("sync", "dbgs", AP(dbg["zt"], 0, [[NH * T, 128], [1, NH * T]]), ZT,
                             waits=later(ev_zt_done))
            RES["dbg"] = dma("sync", "dbgs", AP(dbg["qt"], 0, [[T, 128], [1, T]]), QT, waits=later(ev_zt_done))
            RES["dbg"] = dma("sync", "dbgs", AP(dbg["kt"], 0, [[T, 128], [1, T]]), KT, waits=later(ev_zt_done))
            RES["dbg"] = dma("sync", "dbgs", AP(dbg["vtm"], 0, [[T, 128], [1, T]]), Vtm, waits=later(ev_zt_done))
        if max_phase < 3:
            break

        ln2pend = None

        def flush_ln2():
            nonlocal ln2pend
            if ln2pend is None:
                return
            i_, ev_n2_ = ln2pend
            ln2pend = None
            p_ = i_ % 2
            ev_t2 = None
            for c in range(NDC):
                ev_t2 = tr(pb_pt[:, c * 128:(c + 1) * 128],
                           ln2tms[p_][:, c * 128:(c + 1) * 128],
                           waits=(later(ev_n2_, RES.get("pb_pt")) if c == 0 else ()),
                           inc=("pe" if c == NDC - 1 else None))
            RES["ln2tm%d" % p_] = ev_t2
            S.emit("act", lambda e, j=i_: e.copy(
                AP(A2, j * 128, [[32768, 128], [2048, 8], [1, 128]]),
                pb_pt[:, 0:1024]), waits=later(ev_t2))
            ev_e2 = S.emit("act", lambda e, j=i_: e.copy(
                AP(A2, 8 * 2048 + j * 128, [[32768, 128], [2048, 8], [1, 128]]),
                pb_pt[:, 1024:2048]), inc="act")
            RES["pb_pt"] = ev_e2
            RES["ln2T_done"] = ev_e2

        for i in range(NTT):
            ev_x3 = dma("sync", "dfa", fA[:], AP(x_d, s * T * D + i * 128 * D, [[D, 128], [1, D]]),
                        waits=later(RES.get("fA")))
            ev_o = None
            for h in range(NH):
                for col in range(4):
                    w0 = []
                    if h == 0 and col == 0:
                        w0 = later(RES.get("wo"), RES.get("pb_s"), ev_zt_done)
                    ev_o = mm(pb_s[:, col * 512:(col + 1) * 512],
                              ZT[:, h * 2048 + i * 128: h * 2048 + (i + 1) * 128],
                              wo_sb[:, h * 2048 + col * 512: h * 2048 + (col + 1) * 512],
                              512, h == 0, h == NH - 1, waits=w0,
                              inc=("pe" if (h == NH - 1 and col == 3) else None))
            ev_aoe = S.emit("act", lambda e: e.copy(aoB[:], pb_s[:]),
                            waits=later(ev_o, RES.get("aoB")), inc="act")
            RES["pb_s"] = ev_aoe
            ln_stats([aoB[:, j * 512:(j + 1) * 512] for j in range(4)], waits=later(ev_aoe))
            ev_xb = S.emit("dve", lambda e: e.tensor_scalar(
                fA[:], fA[:], nmr[:], None, op0=ALU.add), waits=later(ev_x3))
            S.emit("dve", lambda e: e.drain())
            ev_y = S.emit("dve", lambda e: e.scalar_tensor_tensor(
                fB[:], aoB[:], rstd[:], fA[:], op0=ALU.mult, op1=ALU.add),
                waits=later(RES.get("fB"), RES.get("fB2")), inc="dve")
            RES["aoB"] = ev_y
            RES["fA"] = ev_y
            ev_yd = dma("sync", "dfb", AP(y_s, s * T * D + i * 128 * D, [[D, 128], [1, D]]), fB[:],
                        waits=later(ev_y))
            RES["fB"] = ev_yd
            ln_stats([fB[:, j * 512:(j + 1) * 512] for j in range(4)], waits=later(ev_y))
            ev_n2 = S.emit("dve", lambda e, p=i % 2: e.tensor_scalar(
                ln2tms[p][:], fB[:], rstd[:], nmr[:], op0=ALU.mult, op1=ALU.add),
                waits=later(RES.get("ln2tm%d" % (i % 2))), inc="dve")
            RES["fB2"] = ev_n2
            flush_ln2()
            ln2pend = (i, ev_n2)
            pump_phw(12)
        flush_ln2()
        RES["wo_free"] = RES["ln2T_done"]
        if debug >= 3 and s == 0:
            RES["dbg"] = dma("sync", "dbgs", AP(dbg["y"], 0, [[1, NSEQ * T * D]]),
                             AP(y_s, 0, [[1, NSEQ * T * D]]), waits=later(RES.get("fB")))
        if max_phase < 4:
            break

        phw_wait = phw_done_ev() if s == 0 else []
        FB = [pb_q[0][:], pb_q[1][:], pb_s[:, 0:512], pb_s[:, 512:1024]]
        RES["fb0"] = RES.get("pb_q0")
        RES["fb1"] = RES.get("pb_q1")
        RES["fb2"] = RES.get("pb_s")
        RES["fb3"] = RES.get("pb_s")
        gdone_prev = {}
        ddone_prev = {}
        for tt in range(4):
            ev_ln2_done = RES.get("ln2T_done")
            qgu, qp = ("sync", 0) if tt % 2 == 0 else ("pool", 1)
            qdn = "pool" if tt % 2 == 0 else "sync"
            ev_gdone = {}
            ev_h_done = None
            for ho in range(NHO):
                sl = ho % 4
                gb, ub = 2 * (ho % 2), 2 * (ho % 2) + 1
                if ho < 4:
                    prev_guard = RES.get("wo_free") if tt == 0 else gdone_prev.get(NHO - 1)
                else:
                    prev_guard = None
                wts = later(*(phw_wait if (s == 0 and tt == 0 and ho < 4) else []),
                            ev_gdone.get(ho - 4), prev_guard)
                ev_wg = dma(qgu, "dg%d%d" % (qp, sl), AP(A3, sl * 4096, [[20480, 128], [1, 2048]]),
                            AP(wg_s, ho * 128 * 2048, [[2048, 128], [1, 2048]]), waits=wts)
                ev_wu = dma(qgu, "dg%d%d" % (qp, sl), AP(A3, sl * 4096 + 2048, [[20480, 128], [1, 2048]]),
                            AP(wu_s, ho * 128 * 2048, [[2048, 128], [1, 2048]]))
                for dc in range(NDC):
                    w0 = []
                    if dc == 0:
                        w0 = later(ev_wg, ev_wu, ev_ln2_done, RES.get("fb%d" % gb))
                    mm(FB[gb], wgu4[sl][:, dc * 128:(dc + 1) * 128],
                       A2[:, dc * 2048 + tt * 512: dc * 2048 + (tt + 1) * 512],
                       512, dc == 0, dc == NDC - 1, waits=w0)
                ev_gu = None
                for dc in range(NDC):
                    w0 = []
                    if dc == 0:
                        w0 = later(RES.get("fb%d" % ub))
                    ev_gu = mm(FB[ub], wgu4[sl][:, 2048 + dc * 128: 2048 + (dc + 1) * 128],
                               A2[:, dc * 2048 + tt * 512: dc * 2048 + (tt + 1) * 512],
                               512, dc == 0, dc == NDC - 1,
                               waits=w0, inc=("pe" if dc == NDC - 1 else None))
                ev_gdone[ho] = ev_gu
                ev_si = S.emit("act", lambda e, b=gb, o=ho: e.activation(
                    out=gtmp[:], in_=FB[b], func=AF.Silu, scale=sgc[:, o:o + 1]),
                    waits=later(ev_gu, RES.get("gtmp")), inc="act")
                RES["fb%d" % gb] = ev_si
                ev_h = S.emit("dve", lambda e, b=ub, o=ho: e.scalar_tensor_tensor(
                    hT[:, o * 512:(o + 1) * 512], FB[b], suc[:, o:o + 1], gtmp[:],
                    op0=ALU.mult, op1=ALU.mult), waits=later(ev_si), inc="dve")
                RES["fb%d" % ub] = ev_h
                RES["gtmp"] = ev_h
                ev_h_done = ev_h
            if debug >= 4 and s == 0 and tt == 0:
                RES["dbg"] = dma("sync", "dbgs", AP(dbg["ht"], 0, [[NHO * 512, 128], [1, NHO * 512]]),
                                 hT, waits=later(ev_h_done))
            ev_ddone = {}
            ev_dT_done = None
            for do in range(NDC):
                sl = do % 2
                b = do % 4
                if do >= 2:
                    slot_guard = ev_ddone.get(do - 2)
                elif tt > 0:
                    slot_guard = ddone_prev.get(NDC - 2 + do)
                else:
                    slot_guard = RES.get("wo_free")
                ev_wd = dma(qdn, "dn%d%d" % (qp, sl), AP(A1, 22528 + sl * 5632, [[34816, 128], [1, 5632]]),
                            AP(wd_s, do * 128 * 5632, [[5632, 128], [1, 5632]]),
                            waits=later(slot_guard))
                ev_d = None
                for hoc in range(NHO):
                    w0 = []
                    if hoc == 0:
                        w0 = later(ev_wd, RES.get("fb%d" % b), ev_h_done)
                    ev_d = mm(FB[b], wdsl[sl][:, hoc * 128:(hoc + 1) * 128],
                              hT[:, hoc * 512:(hoc + 1) * 512], 512, hoc == 0, hoc == NHO - 1,
                              waits=w0, inc=("pe" if hoc == NHO - 1 else None))
                ev_ddone[do] = ev_d
                ev_de = S.emit("act", lambda e, o=do, bb=b, t_=tt: e.activation(
                    out=A2[:, o * 2048 + t_ * 512: o * 2048 + (t_ + 1) * 512], in_=FB[bb],
                    func=AF.Copy,
                    scale=sdc[:, o:o + 1]), waits=later(ev_d), inc="act")
                RES["fb%d" % b] = ev_de
                ev_dT_done = ev_de
            RES["wd_read"] = ev_ddone.get(NDC - 1)
            for i2 in range(4):
                t0 = tt * 512 + i2 * 128
                dm = dtms[i2 % 2]
                ev_dtr = None
                for do in range(NDC):
                    ev_dtr = tr(pb_pt[:, do * 128:(do + 1) * 128],
                                A2[:, do * 2048 + tt * 512 + i2 * 128:
                                   do * 2048 + tt * 512 + (i2 + 1) * 128],
                                waits=(later(ev_dT_done, RES.get("pb_pt"))
                                       if do == 0 else ()),
                                inc=("pe" if do == NDC - 1 else None))
                ev_dtm = S.emit("act", lambda e, d_=dm: e.copy(d_[:], pb_pt[:]),
                                waits=later(ev_dtr, RES.get("dtm%d" % (i2 % 2))), inc="act")
                RES["pb_pt"] = ev_dtm
                ev_y42 = dma("sync", "dfa", fA[:], AP(y_s, s * T * D + t0 * D, [[D, 128], [1, D]]),
                             waits=later(RES.get("fA")))
                ln_stats([dm[:, j * 512:(j + 1) * 512] for j in range(4)],
                         waits=later(ev_dtm))
                S.emit("dve", lambda e: e.tensor_scalar(
                    fA[:], fA[:], nmr[:], None, op0=ALU.add), waits=later(ev_y42))
                S.emit("dve", lambda e: e.drain())
                ev_out = S.emit("dve", lambda e, d_=dm: e.scalar_tensor_tensor(
                    fB[:], d_[:], rstd[:], fA[:], op0=ALU.mult, op1=ALU.add),
                    waits=later(RES.get("fB")), inc="dve")
                RES["fA"] = ev_out
                RES["dtm%d" % (i2 % 2)] = ev_out
                ev_od = dma("sync", "dfb", AP(out_d, s * T * D + t0 * D, [[D, 128], [1, D]]), fB[:],
                            waits=later(ev_out))
                RES["fB"] = ev_od
            gdone_prev = ev_gdone
            ddone_prev = ev_ddone
        RES["pb_q0"] = RES["fb0"]
        RES["pb_q1"] = RES["fb1"]
        RES["pb_s"] = RES["fb3"]
        RES["wgu_free"] = gdone_prev.get(NHO - 1)

    S.emit("sync", lambda e: e.nop(),
           waits=[(s_, S.cnt[s_]) for s_ in semnames
                  if s_ not in ("pe", "act", "dve", "pool") and S.cnt.get(s_, 0) > 0])

    with es:
        with nc.Block() as block:
            secmap = {"sync": block.sync, "pe": block.tensor, "act": block.scalar,
                      "dve": block.vector, "pool": block.gpsimd}
            for eng_name in S.ENGS:
                ops = S.ops[eng_name]
                if not ops:
                    continue

                def make_body(ops_):
                    def body(eng):
                        for fn, ws, inc in ops_:
                            for sem, val in ws:
                                eng.wait_ge(sems[sem], val)
                            inst = fn(eng)
                            if inc is not None:
                                sem, amt = inc
                                if inst is None:
                                    inst = eng.nop()
                                inst.then_inc(sems[sem], amt)
                    return body

                secmap[eng_name](make_body(ops))
    return nc


_BUILD_KW = {"max_phase": 4, "debug": 0, "nheads": NH, "nqt": NTT}


def kernel(L0_attn_x, L0_attn_Wq, L0_attn_Wk, L0_attn_Wv, L0_attn_Wo,
           L0_ffn_wq_gate, L0_ffn_s_gate, L0_ffn_wq_up, L0_ffn_s_up,
           L0_ffn_wq_down, L0_ffn_s_down):
    x = np.ascontiguousarray(np.asarray(L0_attn_x, dtype=np.float32))
    wqT = np.ascontiguousarray(np.asarray(L0_attn_Wq, dtype=np.float32).T)
    wkT = np.ascontiguousarray(np.asarray(L0_attn_Wk, dtype=np.float32).T)
    wvT = np.ascontiguousarray(np.asarray(L0_attn_Wv, dtype=np.float32).T)
    woT = np.ascontiguousarray(np.asarray(L0_attn_Wo, dtype=np.float32).T)
    wgT = np.ascontiguousarray(np.asarray(L0_ffn_wq_gate, dtype=np.int32).T)
    wuT = np.ascontiguousarray(np.asarray(L0_ffn_wq_up, dtype=np.int32).T)
    wdT = np.ascontiguousarray(np.asarray(L0_ffn_wq_down, dtype=np.int32).T)
    sg = np.ascontiguousarray(np.asarray(L0_ffn_s_gate, dtype=np.float32).reshape(NHO, 128).T)
    su = np.ascontiguousarray(np.asarray(L0_ffn_s_up, dtype=np.float32).reshape(NHO, 128).T)
    sd = np.ascontiguousarray(np.asarray(L0_ffn_s_down, dtype=np.float32).reshape(NDC, 128).T)
    ident = np.eye(128, dtype=ml_dtypes.bfloat16)
    mask = np.ascontiguousarray(
        np.triu(np.full((128, 128), NEG, dtype=np.float32), k=1).T.astype(ml_dtypes.bfloat16))

    nc = build_nc(**_BUILD_KW)
    in_maps = []
    for c in range(8):
        in_maps.append({
            "x": x[2 * c:2 * c + 2], "wqT": wqT, "wkT": wkT, "wvT": wvT, "woT": woT,
            "wgT": wgT, "wuT": wuT, "wdT": wdT, "sg": sg, "su": su, "sd": sd,
            "ident": ident, "mask": mask,
        })
    res = run_bass_kernel_spmd(nc, in_maps, core_ids=list(range(8)))
    if _BUILD_KW["debug"] or _BUILD_KW["max_phase"] < 4:
        return res
    return np.concatenate([r["out"] for r in res.results], axis=0)


# version 19
# speedup vs baseline: 1.0662x; 1.0291x over previous
import math
import numpy as np
import ml_dtypes
from contextlib import ExitStack

import concourse.bass as bass
import concourse.mybir as mybir
from concourse.bass_utils import run_bass_kernel_spmd

F32 = mybir.dt.float32
BF16 = mybir.dt.bfloat16
I32 = mybir.dt.int32
AF = mybir.ActivationFunctionType
ALU = mybir.AluOpType
AX = mybir.AxisListType
AP = bass.AP

D = 2048
T = 2048
NSEQ = 2
NH = 16
HD = 128
HID = 5632
NDC = D // 128
NHO = HID // 128
NTT = T // 128
QSCALE = 1.0 / math.sqrt(HD)
NEG = -1.0e30
EPS = 1.0e-5
PACE = 0.0


def pace_n(n):
    return int(math.ceil(PACE * n))


class Sched:
    ENGS = ("sync", "pe", "act", "dve", "pool")

    def __init__(self):
        self.ops = {e: [] for e in self.ENGS}
        self.cnt = {}
        self.last_wait = {}

    def emit(self, eng, fn, waits=(), inc=None, inc_amt=1, force_waits=()):
        def collapse(wl):
            m = {}
            for w in wl:
                if w is None:
                    continue
                s_, v_ = w
                if v_ > m.get(s_, -1):
                    m[s_] = v_
            return [(s_, v_) for s_, v_ in m.items()]
        waits = collapse(waits)
        force_waits = collapse(force_waits)
        ws = []
        for w in force_waits:
            if w is None:
                continue
            sem, val = w
            k = (eng, sem)
            if self.last_wait.get(k, 0) >= val:
                continue
            ws.append((sem, val))
            self.last_wait[k] = val
        for w in waits:
            if w is None:
                continue
            sem, val = w
            if sem == eng:
                continue
            k = (eng, sem)
            if self.last_wait.get(k, 0) >= val:
                continue
            ws.append((sem, val))
            self.last_wait[k] = val
        ev = None
        if inc is not None:
            self.cnt[inc] = self.cnt.get(inc, 0) + inc_amt
            ev = (inc, self.cnt[inc])
        if inc is not None and eng in ("act", "dve"):
            self.ops[eng].append((fn, ws, None))
            self.ops[eng].append((lambda e: e.drain(), [], (inc, inc_amt)))
        else:
            self.ops[eng].append((fn, ws, (inc, inc_amt) if inc else None))
        return ev


def later(*evs):
    return [e for e in evs if e is not None]


def build_nc(max_phase=4, debug=0, nheads=NH, nqt=NTT):
    nc = bass.Bass("TRN2", target_bir_lowering=False)
    S = Sched()

    x_d = nc.dram_tensor("x", [NSEQ, T, D], F32, kind="ExternalInput")
    wqT_d = nc.dram_tensor("wqT", [D, D], F32, kind="ExternalInput")
    wkT_d = nc.dram_tensor("wkT", [D, D], F32, kind="ExternalInput")
    wvT_d = nc.dram_tensor("wvT", [D, D], F32, kind="ExternalInput")
    woT_d = nc.dram_tensor("woT", [D, D], F32, kind="ExternalInput")
    wgT_d = nc.dram_tensor("wgT", [D, HID], I32, kind="ExternalInput")
    wuT_d = nc.dram_tensor("wuT", [D, HID], I32, kind="ExternalInput")
    wdT_d = nc.dram_tensor("wdT", [HID, D], I32, kind="ExternalInput")
    sg_d = nc.dram_tensor("sg", [128, NHO], F32, kind="ExternalInput")
    su_d = nc.dram_tensor("su", [128, NHO], F32, kind="ExternalInput")
    sd_d = nc.dram_tensor("sd", [128, NDC], F32, kind="ExternalInput")
    id_d = nc.dram_tensor("ident", [128, 128], BF16, kind="ExternalInput")
    mk_d = nc.dram_tensor("mask", [128, 128], BF16, kind="ExternalInput")
    out_d = nc.dram_tensor("out", [NSEQ, T, D], F32, kind="ExternalOutput")

    wg_s = nc.dram_tensor("wg_s", [NHO, 128, NDC * 128], BF16)
    wu_s = nc.dram_tensor("wu_s", [NHO, 128, NDC * 128], BF16)
    wd_s = nc.dram_tensor("wd_s", [NDC, 128, NHO * 128], BF16)
    y_s = nc.dram_tensor("y_s", [NSEQ, T, D], F32)

    dbg = {}
    if debug >= 1:
        dbg["ln1"] = nc.dram_tensor("dbg_ln1", [128, NDC * T], BF16, kind="ExternalOutput")
    if debug >= 2:
        dbg["zt"] = nc.dram_tensor("dbg_zt", [128, NH * T], BF16, kind="ExternalOutput")
        dbg["qt"] = nc.dram_tensor("dbg_qt", [128, T], BF16, kind="ExternalOutput")
        dbg["kt"] = nc.dram_tensor("dbg_kt", [128, T], BF16, kind="ExternalOutput")
        dbg["vtm"] = nc.dram_tensor("dbg_vtm", [128, T], BF16, kind="ExternalOutput")
    if debug >= 3:
        dbg["y"] = nc.dram_tensor("dbg_y", [NSEQ, T, D], F32, kind="ExternalOutput")
    if debug >= 4:
        dbg["ln2"] = nc.dram_tensor("dbg_ln2", [128, NDC * 512], BF16, kind="ExternalOutput")
        dbg["ht"] = nc.dram_tensor("dbg_ht", [128, NHO * 512], BF16, kind="ExternalOutput")
        dbg["dt"] = nc.dram_tensor("dbg_dt", [128, NDC * 512], BF16, kind="ExternalOutput")

    es = ExitStack()

    def sb(name, shape, dt):
        return es.enter_context(nc.sbuf_tensor(name, shape, dt))

    def psum(name, shape, dt):
        return es.enter_context(nc.psum_tensor(name, shape, dt))

    A1 = sb("A1", [128, 34816], BF16)
    A2 = sb("A2", [128, 32768], BF16)
    A3 = sb("A3", [128, 20480], BF16)
    fA = sb("fA", [128, D], F32)
    fB = sb("fB", [128, D], F32)
    gtmp = sb("gtmp", [128, 512], F32)
    wint = [sb(f"wint{i}", [128, 1024], I32) for i in range(2)]
    wout = [sb(f"wout{i}", [128, 1024], BF16) for i in range(2)]
    ident = sb("identsb", [128, 128], BF16)
    mask = sb("masksb", [128, 128], BF16)
    zq2 = sb("zqsb", [128, 256], BF16)
    sgc = sb("sgc", [128, NHO], F32)
    suc = sb("suc", [128, NHO], F32)
    sdc = sb("sdc", [128, NDC], F32)
    st6 = sb("st6", [128, 24], F32)
    mv = sb("mv", [128, 2], F32)
    veps = sb("veps", [128, 1], F32)
    sdv = sb("sdv", [128, 1], F32)
    rstd = sb("rstd", [128, 1], F32)
    nmr = sb("nmr", [128, 1], F32)
    negmx = sb("negmx", [128, 1], F32)
    sumr = sb("sumr", [128, 2], F32)
    rs = sb("rs", [128, 2], F32)

    QT = A3[:, 0:2048]
    KT = A3[:, 2048:4096]
    VT = A3[:, 4096:6144]
    Vtm = A3[:, 6144:8192]
    P_sb = A3[:, 8192:10240]
    Pn = A3[:, 10240:12288]
    PT = A3[:, 12288:14336]
    wq_sb = A3[:, 14336:16384]
    wk_sb = A3[:, 16384:18432]
    wv_sb = A3[:, 18432:20480]
    ln1tms = [A3[:, 0:2048], A3[:, 2048:4096]]
    ln2tms = [A3[:, 2048:4096], A3[:, 4096:6144]]
    aoB = A3[:, 0:2048]

    ln1T = A1[:, 0:32768]
    wo_sb = A1[:, 0:32768]
    hT = A1[:, 0:22528]
    wgu4 = [A3[:, j * 4096:(j + 1) * 4096] for j in range(4)]
    dtms = [A3[:, 16384:18432], A3[:, 18432:20480]]
    wdsl = [A1[:, 22528:28160], A1[:, 28160:33792]]

    ZT = A2[:, 0:32768]

    pb_s = psum("pb_s", [128, 2048], F32)
    pb_pt = psum("pb_pt", [128, 2048], BF16)
    pb_q = [psum("pb_q0", [128, 512], F32), psum("pb_q1", [128, 512], F32)]
    pb_z = pb_q[0][:, 0:128]

    sems = {}
    semnames = ["dcs", "dfa", "dfb", "dwi", "dwo", "dc", "dbgs", "pe", "act", "dve", "pool"]
    semnames += ["dg%d%d" % (q, sl) for q in range(2) for sl in range(5)]
    semnames += ["dn%d%d" % (q, sl) for q in range(2) for sl in range(2)]
    semnames += ["dl0", "dl1", "dvt", "dzq0", "dzq1"]
    for s_ in semnames:
        sems[s_] = es.enter_context(nc.semaphore(s_))

    RES = {}

    def mm(out_ap, lhsT_ap, rhs_ap, n, start, stop, waits=(), inc=None):
        ev = S.emit("pe", lambda e, o=out_ap, l=lhsT_ap, r=rhs_ap, st=start, sp=stop:
                    e.matmul(o, l, r, start=st, stop=sp), waits=waits, inc=inc)
        c = pace_n(n)
        if c:
            S.emit("pe", lambda e, c_=c: e.nop(cycle_cnt=c_))
        return ev

    def tr(out_ap, in_ap, waits=(), inc=None):
        ev = S.emit("pe", lambda e, o=out_ap, i=in_ap: e.transpose(o, i, ident[:]),
                    waits=waits, inc=inc)
        c = pace_n(128)
        if c:
            S.emit("pe", lambda e, c_=c: e.nop(cycle_cnt=c_))
        return ev

    def dma(eng, sem, dst, src, waits=(), force_waits=()):
        return S.emit(eng, lambda e, d=dst, s=src: e.dma_start(out=d, in_=s),
                      waits=waits, force_waits=force_waits, inc=sem, inc_amt=16)

    def dmat(eng, sem, dst, src, waits=(), force_waits=()):
        return S.emit(eng, lambda e, d=dst, s=src: e.dma_start_transpose(out=d, in_=s),
                      waits=waits, force_waits=force_waits, inc=sem, inc_amt=16)

    def ln_stats(src4, waits=()):
        for j, sap in enumerate(src4):
            S.emit("dve", lambda e, s_=sap, d_=st6[:, j * 6:(j + 1) * 6]: e.bn_stats(d_, s_),
                   waits=(waits if j == 0 else ()))
        S.emit("dve", lambda e: e.drain())
        S.emit("dve", lambda e: e.bn_aggr(mv[:], st6[:, 0:6 * len(src4)]))
        S.emit("dve", lambda e: e.drain())
        ev_veps = S.emit("dve", lambda e: e.tensor_scalar(veps[:], mv[:, 1:2], EPS, None, op0=ALU.add),
                         inc="dve")
        ev_sd = S.emit("act", lambda e: e.sqrt(sdv[:], veps[:]), waits=[ev_veps], inc="act")
        S.emit("dve", lambda e: e.reciprocal(rstd[:], sdv[:]), waits=[ev_sd])
        S.emit("dve", lambda e: e.drain())
        S.emit("dve", lambda e: e.tensor_scalar(nmr[:], mv[:, 0:1], rstd[:], -1.0,
                                                op0=ALU.mult, op1=ALU.mult))
        ev = S.emit("dve", lambda e: e.drain(), inc="dve")
        return ev

    phw_units = []

    def make_phw_units():
        for (src_d, dst_d) in ((wgT_d, wg_s), (wuT_d, wu_s)):
            for dc in range(NDC):
                for cg in range(HID // 1024 + 1):
                    wcols = min(1024, HID - cg * 1024)
                    src = AP(src_d, dc * 128 * HID + cg * 1024, [[HID, 128], [1, wcols]])
                    dst = AP(dst_d, 8 * cg * (128 * 2048) + dc * 128,
                             [[2048, 128], [128 * 2048, wcols // 128], [1, 128]])
                    phw_units.append((src, dst, wcols))
        for hoc in range(NHO):
            for dou in range(D // 1024):
                src = AP(wdT_d, hoc * 128 * D + dou * 1024, [[D, 128], [1, 1024]])
                dst = AP(wd_s, 8 * dou * (128 * 5632) + hoc * 128,
                         [[5632, 128], [128 * 5632, 8], [1, 128]])
                phw_units.append((src, dst, 1024))

    make_phw_units()
    phw_state = {"i": 0, "conv_ev": {}, "out_ev": {}}

    def pump_phw(k):
        for _ in range(k):
            i = phw_state["i"]
            if i >= len(phw_units):
                return
            phw_state["i"] = i + 1
            src, dst, wcols = phw_units[i]
            par = i % 2
            ev_in = dma("pool", "dwi", wint[par][:, 0:wcols], src,
                        waits=later(phw_state["conv_ev"].get(i - 2)))
            ev_cv = S.emit("pool",
                           lambda e, p=par, w=wcols: e.tensor_scalar(
                               wout[p][:, 0:w], wint[p][:, 0:w], -128.0, 1.0,
                               op0=ALU.add, op1=ALU.mult),
                           waits=later(ev_in, phw_state["out_ev"].get(i - 1)), inc="pool")
            phw_state["conv_ev"][i] = ev_cv
            ev_out = dma("pool", "dwo", dst, wout[par][:, 0:wcols], force_waits=later(ev_cv))
            phw_state["out_ev"][i] = ev_out

    def phw_done_ev():
        pump_phw(len(phw_units))
        n = len(phw_units)
        return later(phw_state["out_ev"].get(n - 1), phw_state["out_ev"].get(n - 2))

    ev_init = None
    if debug:
        for t_ in (A1, A2, A3):
            ev_init = S.emit("pool", lambda e, t__=t_: e.memset(t__[:, :], 0), inc="pool")

    for dst, src in ((ident, id_d), (mask, mk_d), (sgc, sg_d), (suc, su_d), (sdc, sd_d)):
        ev_consts = dma("sync", "dcs", dst[:], AP(src, 0, [[src.shape[1], 128], [1, src.shape[1]]]))

    def cast_head(h, waits):
        ev = None
        for (wsb, wdr) in ((wq_sb, wqT_d), (wk_sb, wkT_d), (wv_sb, wvT_d)):
            for g in range(2):
                src = AP(wdr, h * 128 + g * 8 * 128 * D, [[D, 128], [128 * D, 8], [1, 128]])
                dst = AP(wsb.tensor, wsb.offset + g * 8 * 128,
                         [[20480, 128], [128, 8], [1, 128]])
                ev = dma("pool", "dc", dst, src, waits=(waits if g == 0 else ()))
        return ev

    for s in range(NSEQ):
        RES["cast0"] = cast_head(0, later(RES.get("wgu_free"), RES.get("dtm0"),
                                          RES.get("dtm1")))
        ln1pend = None

        def flush_ln1():
            nonlocal ln1pend
            if ln1pend is None:
                return
            i_, ev_nm_ = ln1pend
            ln1pend = None
            p_ = i_ % 2
            ev_tp = dmat("sync", "dl%d" % p_,
                         AP(ln1T.tensor, ln1T.offset + i_ * 128,
                            [[34816, 128], [2048, NDC], [1, 128]]),
                         ln1tms[p_][:],
                         waits=later(ev_nm_, RES.get("wd_read")))
            RES["ln1tm%d" % p_] = ev_tp
            RES["ln1dma"] = ev_tp

        for i in range(NTT):
            xb_ = fA if i % 2 == 0 else fB
            ev_x = dma("sync", "dfa" if i % 2 == 0 else "dfb", xb_[:],
                       AP(x_d, s * T * D + i * 128 * D, [[D, 128], [1, D]]),
                       waits=later(RES.get("fA" if i % 2 == 0 else "fB")))
            pump_phw(3)
            ev_nm = ln_stats([xb_[:, j * 512:(j + 1) * 512] for j in range(4)],
                             waits=later(ev_x))
            ev_norm = S.emit("dve", lambda e, b_=xb_, p=i % 2: e.tensor_scalar(
                ln1tms[p][:], b_[:], rstd[:], nmr[:], op0=ALU.mult, op1=ALU.add),
                waits=later(RES.get("ln1tm%d" % (i % 2)), RES.get("wgu_free"), ev_init),
                inc="dve")
            RES["fA" if i % 2 == 0 else "fB"] = ev_norm
            flush_ln1()
            ln1pend = (i, ev_norm)
        flush_ln1()
        ev_ln1_done = later(RES.get("ln1tm0"), RES.get("ln1tm1"))
        if debug >= 1 and s == 0:
            RES["dbg"] = dma("sync", "dbgs", AP(dbg["ln1"], 0, [[NDC * T, 128], [1, NDC * T]]),
                             ln1T, waits=later(*ev_ln1_done))
        if max_phase < 2:
            break

        ev_cast = {}

        ev_cast[0] = RES["cast0"]
        for h in range(nheads):
            ev_qkv_banks = [RES.get("pb_q0"), RES.get("pb_q1")]
            ev_evacs = []
            for widx, (wsb, dstT, scale) in enumerate(
                    ((wq_sb, QT, QSCALE), (wk_sb, KT, None), (wv_sb, VT, None))):
                for tc in range(4):
                    b = (widx * 4 + tc) % 2
                    ev_g = None
                    for dc in range(NDC):
                        w0 = []
                        if dc == 0:
                            w0 = later(ev_cast[h], ev_qkv_banks[b],
                                       *(ev_ln1_done if h == 0 else []))
                        ev_g = mm(pb_q[b][:], wsb[:, dc * 128:(dc + 1) * 128],
                                  ln1T[:, dc * 2048 + tc * 512: dc * 2048 + (tc + 1) * 512],
                                  512, dc == 0, dc == NDC - 1, waits=w0,
                                  inc=("pe" if dc == NDC - 1 else None))
                    if scale is None:
                        ev_e = S.emit("act", lambda e, d=dstT[:, tc * 512:(tc + 1) * 512],
                                      p=pb_q[b][:]: e.copy(d, p), waits=later(ev_g), inc="act")
                    else:
                        ev_e = S.emit("act", lambda e, d=dstT[:, tc * 512:(tc + 1) * 512],
                                      p=pb_q[b][:], sc=scale: e.activation(
                                          out=d, in_=p, func=AF.Copy, scale=sc),
                                      waits=later(ev_g), inc="act")
                    ev_qkv_banks[b] = ev_e
                    ev_evacs.append(ev_e)
            RES["pb_q0"] = ev_qkv_banks[0]
            RES["pb_q1"] = ev_qkv_banks[1]
            ev_vt_done = ev_evacs[-1]
            if h + 1 < nheads:
                ev_cast[h + 1] = cast_head(h + 1, later(ev_evacs[-1]))
            else:
                for g in range(4):
                    src = AP(woT_d, g * 4 * 128 * D, [[D, 128], [128 * D, 4], [1, D]])
                    dst = AP(A1, g * 4 * 2048, [[34816, 128], [2048, 4], [1, D]])
                    RES["wo"] = dma("pool", "dc", dst, src,
                                    waits=(later(ev_evacs[-1], RES.get("dbg"))
                                           if g == 0 else ()))
            ev_vtm = dmat("sync", "dvt",
                          AP(A3, 6144, [[20480, 128], [128, NDC], [1, 128]]), VT[:],
                          waits=later(ev_vt_done, RES.get("avlast")))
            RES["vtm_dma"] = ev_vtm
            ev_qk_ready = later(ev_evacs[3], ev_evacs[7])
            PB = [P_sb, Pn]
            pend = None

            def emit_ptav(qt, ev_exp, ev_rc):
                buf = PB[qt % 2]
                par = qt % 2
                nch_k = qt + 1
                h1 = (nch_k + 1) // 2
                ev_pt1 = None
                for k in range(h1):
                    ev_pt1 = tr(pb_pt[:, k * 128:(k + 1) * 128], buf[:, k * 128:(k + 1) * 128],
                                waits=(later(ev_exp, RES.get("pb_pt")) if k == 0 else ()),
                                inc=("pe" if k == h1 - 1 else None))
                ev_pte1 = S.emit("dve", lambda e, k=h1 * 128: e.tensor_copy(
                    PT[:, 0:k], pb_pt[:, 0:k]), waits=later(ev_pt1), inc="dve")
                ev_pt2 = None
                for k in range(h1, nch_k):
                    ev_pt2 = tr(pb_pt[:, k * 128:(k + 1) * 128], buf[:, k * 128:(k + 1) * 128],
                                inc=("pe" if k == nch_k - 1 else None))
                RES["ptav%d" % (qt % 2)] = ev_pt2 if ev_pt2 is not None else ev_pt1
                ev_pte2 = ev_pte1
                if ev_pt2 is not None:
                    ev_pte2 = S.emit("dve", lambda e, a=h1 * 128, b=nch_k * 128: e.tensor_copy(
                        PT[:, a:b], pb_pt[:, a:b]), waits=later(ev_pt2), inc="dve")
                RES["pb_pt"] = ev_pte2
                ev_av = None
                for k in range(nch_k):
                    w0 = ()
                    if k == 0:
                        w0 = later(ev_pte1, RES.get("vtm_dma"), RES.get("pb_q0"))
                    elif k == h1:
                        w0 = later(ev_pte2)
                    ev_av = mm(pb_z, PT[:, k * 128:(k + 1) * 128], Vtm[:, k * 128:(k + 1) * 128],
                               128, k == 0, k == nch_k - 1,
                               waits=w0, inc=("pe" if k == nch_k - 1 else None))
                RES["avlast"] = ev_av
                ev_zq = S.emit("dve", lambda e, p=par: e.tensor_scalar(
                    zq2[:, p * 128:(p + 1) * 128], pb_z, rs[:, p:p + 1], None, op0=ALU.mult),
                    waits=later(ev_av, ev_rc, RES.get("zq%d" % par)), inc="dve")
                RES["pb_q0"] = ev_zq
                RES["rs%d" % par] = ev_zq
                ev_ze = dmat("sync", "dzq%d" % par,
                             AP(A2, h * 2048 + qt * 128, [[32768, 128], [1, 128]]),
                             zq2[:, par * 128:(par + 1) * 128],
                             waits=later(ev_zq))
                RES["zq%d" % par] = ev_ze
                RES["ztd%d" % par] = ev_ze

            for qt in range(nqt):
                kend = (qt + 1) * 128
                par = qt % 2
                buf = PB[par]
                nfull = qt // 4
                ev_s = None
                w0 = later(RES.get("pb_s"), ev_consts, *ev_qk_ready)
                for c in range(nfull):
                    ev_s = mm(pb_s[:, c * 512:(c + 1) * 512], QT[:, qt * 128:(qt + 1) * 128],
                              KT[:, c * 512:(c + 1) * 512], 512, True, True,
                              waits=(w0 if c == 0 else ()))
                rem = qt * 128 - nfull * 512
                if rem:
                    ev_s = mm(pb_s[:, nfull * 512: nfull * 512 + rem],
                              QT[:, qt * 128:(qt + 1) * 128],
                              KT[:, nfull * 512: nfull * 512 + rem], rem, True, True,
                              waits=(w0 if nfull == 0 else ()))
                mm(pb_s[:, qt * 128:kend], QT[:, qt * 128:(qt + 1) * 128],
                   KT[:, qt * 128:kend], 128, True, False,
                   waits=(w0 if (nfull == 0 and rem == 0) else ()))
                ev_s = S.emit("pe", lambda e, q=qt: e.matmul(
                    pb_s[:, q * 128:(q + 1) * 128], mask[:], ident[:],
                    start=False, stop=True), inc="pe")
                ev_mx = S.emit("dve", lambda e, k=kend: e.reduce_max(
                    negmx[:], pb_s[:, 0:k], axis=AX.X, negate=True),
                    waits=later(ev_s), inc="dve")
                ev_exp = S.emit("act", lambda e, k=kend, b=buf, p=par: e.activation(
                    out=b[:, 0:k], in_=pb_s[:, 0:k], func=AF.Exp, bias=negmx[:],
                    accum_out=sumr[:, p:p + 1]),
                    waits=later(ev_mx, RES.get("ptav%d" % par),
                                RES.get("sumr%d" % par)), inc="act")
                RES["pb_s"] = ev_exp
                if pend is not None:
                    emit_ptav(*pend)
                ev_rc = S.emit("dve", lambda e, p=par: e.reciprocal(
                    rs[:, p:p + 1], sumr[:, p:p + 1]),
                    waits=later(ev_exp, RES.get("rs%d" % par)), inc="dve")
                RES["sumr%d" % par] = ev_rc
                pend = (qt, ev_exp, ev_rc)
            emit_ptav(*pend)
            pend = None
            pump_phw(14)
        ev_zt_done = later(RES.get("ztd0"), RES.get("ztd1"))
        if debug >= 2 and s == 0:
            RES["dbg"] = dma("sync", "dbgs", AP(dbg["zt"], 0, [[NH * T, 128], [1, NH * T]]), ZT,
                             waits=later(*ev_zt_done))
            RES["dbg"] = dma("sync", "dbgs", AP(dbg["qt"], 0, [[T, 128], [1, T]]), QT, waits=later(*ev_zt_done))
            RES["dbg"] = dma("sync", "dbgs", AP(dbg["kt"], 0, [[T, 128], [1, T]]), KT, waits=later(*ev_zt_done))
            RES["dbg"] = dma("sync", "dbgs", AP(dbg["vtm"], 0, [[T, 128], [1, T]]), Vtm, waits=later(*ev_zt_done))
        if max_phase < 3:
            break

        ln2pend = None

        def flush_ln2():
            nonlocal ln2pend
            if ln2pend is None:
                return
            i_, ev_n2_, ev_rd_ = ln2pend
            ln2pend = None
            p_ = i_ % 2
            ev_t2 = dmat("sync", "dl%d" % p_,
                         AP(A2, i_ * 128, [[32768, 128], [2048, NDC], [1, 128]]),
                         ln2tms[p_][:],
                         waits=later(ev_n2_, ev_rd_))
            RES["ln2tm%d" % p_] = ev_t2
            RES["ln2T_done%d" % p_] = ev_t2

        for i in range(NTT):
            ev_x3 = dma("sync", "dfa", fA[:], AP(x_d, s * T * D + i * 128 * D, [[D, 128], [1, D]]),
                        waits=later(RES.get("fA")))
            ev_o = None
            for h in range(NH):
                for col in range(4):
                    w0 = []
                    if h == 0 and col == 0:
                        w0 = later(RES.get("wo"), RES.get("pb_s"), *ev_zt_done)
                    ev_o = mm(pb_s[:, col * 512:(col + 1) * 512],
                              ZT[:, h * 2048 + i * 128: h * 2048 + (i + 1) * 128],
                              wo_sb[:, h * 2048 + col * 512: h * 2048 + (col + 1) * 512],
                              512, h == 0, h == NH - 1, waits=w0,
                              inc=("pe" if (h == NH - 1 and col == 3) else None))
            ev_aoe = S.emit("act", lambda e: e.copy(aoB[:], pb_s[:]),
                            waits=later(ev_o, RES.get("aoB")), inc="act")
            RES["pb_s"] = ev_aoe
            ln_stats([aoB[:, j * 512:(j + 1) * 512] for j in range(4)], waits=later(ev_aoe))
            ev_xb = S.emit("dve", lambda e: e.tensor_scalar(
                fA[:], fA[:], nmr[:], None, op0=ALU.add), waits=later(ev_x3))
            S.emit("dve", lambda e: e.drain())
            ev_y = S.emit("dve", lambda e: e.scalar_tensor_tensor(
                fB[:], aoB[:], rstd[:], fA[:], op0=ALU.mult, op1=ALU.add),
                waits=later(RES.get("fB"), RES.get("fB2")), inc="dve")
            RES["aoB"] = ev_y
            RES["fA"] = ev_y
            ev_yd = dma("sync", "dfb", AP(y_s, s * T * D + i * 128 * D, [[D, 128], [1, D]]), fB[:],
                        waits=later(ev_y))
            RES["fB"] = ev_yd
            ln_stats([fB[:, j * 512:(j + 1) * 512] for j in range(4)], waits=later(ev_y))
            ev_n2 = S.emit("dve", lambda e, p=i % 2: e.tensor_scalar(
                ln2tms[p][:], fB[:], rstd[:], nmr[:], op0=ALU.mult, op1=ALU.add),
                waits=later(RES.get("ln2tm%d" % (i % 2))), inc="dve")
            RES["fB2"] = ev_n2
            flush_ln2()
            ln2pend = (i, ev_n2, ev_aoe)
            pump_phw(12)
        flush_ln2()
        RES["wo_free"] = RES.get("pb_s")
        if debug >= 3 and s == 0:
            RES["dbg"] = dma("sync", "dbgs", AP(dbg["y"], 0, [[1, NSEQ * T * D]]),
                             AP(y_s, 0, [[1, NSEQ * T * D]]), waits=later(RES.get("fB")))
        if max_phase < 4:
            break

        phw_wait = phw_done_ev() if s == 0 else []
        FB = [pb_q[0][:], pb_q[1][:], pb_s[:, 0:512], pb_s[:, 512:1024]]
        RES["fb0"] = RES.get("pb_q0")
        RES["fb1"] = RES.get("pb_q1")
        RES["fb2"] = RES.get("pb_s")
        RES["fb3"] = RES.get("pb_s")
        gdone_prev = {}
        ddone_prev = {}
        for tt in range(4):
            ln2_done = later(RES.get("ln2T_done0"), RES.get("ln2T_done1"))
            qgu, qp = ("sync", 0) if tt % 2 == 0 else ("pool", 1)
            qdn = "pool" if tt % 2 == 0 else "sync"
            ev_gdone = {}
            ev_h_done = None
            for ho in range(NHO):
                sl = ho % 4
                gb, ub = 2 * (ho % 2), 2 * (ho % 2) + 1
                if ho < 4:
                    if tt == 0:
                        pg = later(RES.get("wo_free"), RES.get("ln2tm0"),
                                   RES.get("ln2tm1"), RES.get("ln1tm0"),
                                   RES.get("ln1tm1"))
                    else:
                        pg = later(gdone_prev.get(NHO - 1))
                else:
                    pg = []
                wts = later(*(phw_wait if (s == 0 and tt == 0 and ho < 4) else []),
                            ev_gdone.get(ho - 4), *pg)
                ev_wg = dma(qgu, "dg%d%d" % (qp, sl), AP(A3, sl * 4096, [[20480, 128], [1, 2048]]),
                            AP(wg_s, ho * 128 * 2048, [[2048, 128], [1, 2048]]), waits=wts)
                ev_wu = dma(qgu, "dg%d%d" % (qp, sl), AP(A3, sl * 4096 + 2048, [[20480, 128], [1, 2048]]),
                            AP(wu_s, ho * 128 * 2048, [[2048, 128], [1, 2048]]))
                for dc in range(NDC):
                    w0 = []
                    if dc == 0:
                        w0 = later(ev_wg, ev_wu, *ln2_done, RES.get("fb%d" % gb))
                    mm(FB[gb], wgu4[sl][:, dc * 128:(dc + 1) * 128],
                       A2[:, dc * 2048 + tt * 512: dc * 2048 + (tt + 1) * 512],
                       512, dc == 0, dc == NDC - 1, waits=w0)
                ev_gu = None
                for dc in range(NDC):
                    w0 = []
                    if dc == 0:
                        w0 = later(RES.get("fb%d" % ub))
                    ev_gu = mm(FB[ub], wgu4[sl][:, 2048 + dc * 128: 2048 + (dc + 1) * 128],
                               A2[:, dc * 2048 + tt * 512: dc * 2048 + (tt + 1) * 512],
                               512, dc == 0, dc == NDC - 1,
                               waits=w0, inc=("pe" if dc == NDC - 1 else None))
                ev_gdone[ho] = ev_gu
                ev_si = S.emit("act", lambda e, b=gb, o=ho: e.activation(
                    out=gtmp[:], in_=FB[b], func=AF.Silu, scale=sgc[:, o:o + 1]),
                    waits=later(ev_gu, RES.get("gtmp")), inc="act")
                RES["fb%d" % gb] = ev_si
                ev_h = S.emit("dve", lambda e, b=ub, o=ho: e.scalar_tensor_tensor(
                    hT[:, o * 512:(o + 1) * 512], FB[b], suc[:, o:o + 1], gtmp[:],
                    op0=ALU.mult, op1=ALU.mult), waits=later(ev_si), inc="dve")
                RES["fb%d" % ub] = ev_h
                RES["gtmp"] = ev_h
                ev_h_done = ev_h
            if debug >= 4 and s == 0 and tt == 0:
                RES["dbg"] = dma("sync", "dbgs", AP(dbg["ht"], 0, [[NHO * 512, 128], [1, NHO * 512]]),
                                 hT, waits=later(ev_h_done))
            ev_ddone = {}
            ev_dT_done = None
            for do in range(NDC):
                sl = do % 2
                b = do % 4
                if do >= 2:
                    slot_guard = ev_ddone.get(do - 2)
                elif tt > 0:
                    slot_guard = ddone_prev.get(NDC - 2 + do)
                else:
                    slot_guard = RES.get("wo_free")
                ev_wd = dma(qdn, "dn%d%d" % (qp, sl), AP(A1, 22528 + sl * 5632, [[34816, 128], [1, 5632]]),
                            AP(wd_s, do * 128 * 5632, [[5632, 128], [1, 5632]]),
                            waits=later(slot_guard))
                ev_d = None
                for hoc in range(NHO):
                    w0 = []
                    if hoc == 0:
                        w0 = later(ev_wd, RES.get("fb%d" % b), ev_h_done)
                    ev_d = mm(FB[b], wdsl[sl][:, hoc * 128:(hoc + 1) * 128],
                              hT[:, hoc * 512:(hoc + 1) * 512], 512, hoc == 0, hoc == NHO - 1,
                              waits=w0, inc=("pe" if hoc == NHO - 1 else None))
                ev_ddone[do] = ev_d
                ev_de = S.emit("act", lambda e, o=do, bb=b, t_=tt: e.activation(
                    out=A2[:, o * 2048 + t_ * 512: o * 2048 + (t_ + 1) * 512], in_=FB[bb],
                    func=AF.Copy,
                    scale=sdc[:, o:o + 1]), waits=later(ev_d), inc="act")
                RES["fb%d" % b] = ev_de
                ev_dT_done = ev_de
            RES["wd_read"] = ev_ddone.get(NDC - 1)
            for i2 in range(4):
                t0 = tt * 512 + i2 * 128
                dm = dtms[i2 % 2]
                ev_dtr = None
                for do in range(NDC):
                    ev_dtr = tr(pb_pt[:, do * 128:(do + 1) * 128],
                                A2[:, do * 2048 + tt * 512 + i2 * 128:
                                   do * 2048 + tt * 512 + (i2 + 1) * 128],
                                waits=(later(ev_dT_done, RES.get("pb_pt"))
                                       if do == 0 else ()),
                                inc=("pe" if do == NDC - 1 else None))
                ev_dtm = S.emit("act", lambda e, d_=dm: e.copy(d_[:], pb_pt[:]),
                                waits=later(ev_dtr, RES.get("dtm%d" % (i2 % 2))), inc="act")
                RES["pb_pt"] = ev_dtm
                ev_y42 = dma("sync", "dfa", fA[:], AP(y_s, s * T * D + t0 * D, [[D, 128], [1, D]]),
                             waits=later(RES.get("fA")))
                ln_stats([dm[:, j * 512:(j + 1) * 512] for j in range(4)],
                         waits=later(ev_dtm))
                S.emit("dve", lambda e: e.tensor_scalar(
                    fA[:], fA[:], nmr[:], None, op0=ALU.add), waits=later(ev_y42))
                S.emit("dve", lambda e: e.drain())
                ev_out = S.emit("dve", lambda e, d_=dm: e.scalar_tensor_tensor(
                    fB[:], d_[:], rstd[:], fA[:], op0=ALU.mult, op1=ALU.add),
                    waits=later(RES.get("fB")), inc="dve")
                RES["fA"] = ev_out
                RES["dtm%d" % (i2 % 2)] = ev_out
                ev_od = dma("sync", "dfb", AP(out_d, s * T * D + t0 * D, [[D, 128], [1, D]]), fB[:],
                            waits=later(ev_out))
                RES["fB"] = ev_od
            gdone_prev = ev_gdone
            ddone_prev = ev_ddone
        RES["pb_q0"] = RES["fb0"]
        RES["pb_q1"] = RES["fb1"]
        RES["pb_s"] = RES["fb3"]
        RES["wgu_free"] = gdone_prev.get(NHO - 1)

    S.emit("sync", lambda e: e.nop(),
           waits=[(s_, S.cnt[s_]) for s_ in semnames
                  if s_ not in ("pe", "act", "dve", "pool") and S.cnt.get(s_, 0) > 0])

    with es:
        with nc.Block() as block:
            secmap = {"sync": block.sync, "pe": block.tensor, "act": block.scalar,
                      "dve": block.vector, "pool": block.gpsimd}
            for eng_name in S.ENGS:
                ops = S.ops[eng_name]
                if not ops:
                    continue

                def make_body(ops_):
                    def body(eng):
                        for fn, ws, inc in ops_:
                            for sem, val in ws:
                                eng.wait_ge(sems[sem], val)
                            inst = fn(eng)
                            if inc is not None:
                                sem, amt = inc
                                if inst is None:
                                    inst = eng.nop()
                                inst.then_inc(sems[sem], amt)
                    return body

                secmap[eng_name](make_body(ops))
    return nc


_BUILD_KW = {"max_phase": 4, "debug": 0, "nheads": NH, "nqt": NTT}


def kernel(L0_attn_x, L0_attn_Wq, L0_attn_Wk, L0_attn_Wv, L0_attn_Wo,
           L0_ffn_wq_gate, L0_ffn_s_gate, L0_ffn_wq_up, L0_ffn_s_up,
           L0_ffn_wq_down, L0_ffn_s_down):
    x = np.ascontiguousarray(np.asarray(L0_attn_x, dtype=np.float32))
    wqT = np.ascontiguousarray(np.asarray(L0_attn_Wq, dtype=np.float32).T)
    wkT = np.ascontiguousarray(np.asarray(L0_attn_Wk, dtype=np.float32).T)
    wvT = np.ascontiguousarray(np.asarray(L0_attn_Wv, dtype=np.float32).T)
    woT = np.ascontiguousarray(np.asarray(L0_attn_Wo, dtype=np.float32).T)
    wgT = np.ascontiguousarray(np.asarray(L0_ffn_wq_gate, dtype=np.int32).T)
    wuT = np.ascontiguousarray(np.asarray(L0_ffn_wq_up, dtype=np.int32).T)
    wdT = np.ascontiguousarray(np.asarray(L0_ffn_wq_down, dtype=np.int32).T)
    sg = np.ascontiguousarray(np.asarray(L0_ffn_s_gate, dtype=np.float32).reshape(NHO, 128).T)
    su = np.ascontiguousarray(np.asarray(L0_ffn_s_up, dtype=np.float32).reshape(NHO, 128).T)
    sd = np.ascontiguousarray(np.asarray(L0_ffn_s_down, dtype=np.float32).reshape(NDC, 128).T)
    ident = np.eye(128, dtype=ml_dtypes.bfloat16)
    mask = np.ascontiguousarray(
        np.triu(np.full((128, 128), NEG, dtype=np.float32), k=1).T.astype(ml_dtypes.bfloat16))

    nc = build_nc(**_BUILD_KW)
    in_maps = []
    for c in range(8):
        in_maps.append({
            "x": x[2 * c:2 * c + 2], "wqT": wqT, "wkT": wkT, "wvT": wvT, "woT": woT,
            "wgT": wgT, "wuT": wuT, "wdT": wdT, "sg": sg, "su": su, "sd": sd,
            "ident": ident, "mask": mask,
        })
    res = run_bass_kernel_spmd(nc, in_maps, core_ids=list(range(8)))
    if _BUILD_KW["debug"] or _BUILD_KW["max_phase"] < 4:
        return res
    return np.concatenate([r["out"] for r in res.results], axis=0)
